# version 01
# baseline (speedup 1.0000x reference)
import math
import numpy as np
from contextlib import ExitStack

import concourse.bass as bass
import concourse.mybir as mybir
from concourse.bass_utils import run_bass_kernel_spmd

F32 = mybir.dt.float32
BF16 = mybir.dt.bfloat16
I32 = mybir.dt.int32
AF = mybir.ActivationFunctionType
ALU = mybir.AluOpType
AX = mybir.AxisListType

D = 1024
NCH = 8
B = 4
S = 4096
HALF = S // 2
TWO_PI = 2.0 * math.pi

NCORES = 8

SIMSAFE = False



def fm(x):
    return np.ascontiguousarray(np.asarray(x, np.float32).T)


def alibi_slopes(n_heads=16):
    def pow2(n):
        start = 2 ** (-(2 ** (-(math.log2(n) - 3))))
        return np.array([start * start ** i for i in range(n)], dtype=np.float32)
    if math.log2(n_heads).is_integer():
        return pow2(n_heads)
    closest = 2 ** math.floor(math.log2(n_heads))
    sl = pow2(closest)
    extra = pow2(2 * closest)[0::2][: n_heads - closest]
    return np.concatenate([sl, extra], axis=0)


def core_perm(h):
    own = np.arange(HALF) + h * HALF
    oth = np.arange(HALF) + (1 - h) * HALF
    return np.concatenate([own, oth])



class P:
    def __init__(self, nc, stack):
        self.nc = nc
        self.stack = stack
        self.block = stack.enter_context(nc.Block())
        self.sems = {}
        self.cnt = {}
        for name in ("ld", "g", "pe", "act", "dve", "pool", "st"):
            self.sems[name] = stack.enter_context(nc.semaphore(name))
            self.cnt[name] = 0

    def sb(self, name, shape, dtype):
        return self.stack.enter_context(self.nc.sbuf_tensor(name, shape, dtype))

    def psum(self, name, shape, dtype=F32):
        return self.stack.enter_context(self.nc.psum_tensor(name, shape, dtype))

    def inc(self, instr, sem, amt):
        instr.then_inc(self.sems[sem], amt)
        self.cnt[sem] += amt
        return self.cnt[sem]

    def dma(self, eng, out, in_, sem, **kw):
        if SIMSAFE:
            eng.wait_ge(self.sems[sem], self.cnt[sem])
        return self.inc(eng.dma_start(out=out, in_=in_, **kw), sem, 16)

    def tick(self, instr, sem):
        return self.inc(instr, sem, 1)

    def dep(self, eng, instr, sem):
        val = self.inc(instr, sem, 1)
        eng.wait_ge(self.sems[sem], val)
        return val

    def wait(self, eng, sem, val):
        eng.wait_ge(self.sems[sem], val)

    def snap(self):
        return dict(self.cnt)

    def wait_all(self, eng, snap):
        for name, v in snap.items():
            if v > 0:
                eng.wait_ge(self.sems[name], v)

    def barrier(self, sections=None):
        snap = self.snap()
        b = self.block
        for mk in (sections or [b.sync, b.gpsimd, b.tensor, b.vector, b.scalar]):
            @mk
            def _(eng, snap=snap):
                self.wait_all(eng, snap)


class NS:
    pass


CONST_ARRAYS = {
    "ones_col": np.ones((128, 1), np.float32),
    "ones_row": np.ones((1, 128), np.float32),
}


def declare_consts(nc):
    d = {}
    d["ones_col"] = nc.dram_tensor("ones_col", [128, 1], F32, kind="ExternalInput").ap()
    d["ones_row"] = nc.dram_tensor("ones_row", [1, 128], F32, kind="ExternalInput").ap()
    return d


def load_consts(p, dram):
    c = NS()
    c.ones_col = p.sb("c_ones_col", [128, 1], BF16)
    c.ones_row = p.sb("c_ones_row", [1, 128], BF16)
    c.ones_colf = p.sb("c_ones_colf", [128, 1], F32)
    c.ones_rowf = p.sb("c_ones_rowf", [1, 128], F32)

    @p.block.gpsimd
    def _(g):
        p.dma(g, c.ones_col[:], dram["ones_col"][:, :], "g")
        p.dma(g, c.ones_row[:], dram["ones_row"][:, :], "g")
        p.dma(g, c.ones_colf[:], dram["ones_col"][:, :], "g")
        p.dma(g, c.ones_rowf[:], dram["ones_row"][:, :], "g")
    p.barrier()
    return c


def emit_cast_weight(p, src_ap, name):
    nc = p.nc
    shape = list(src_ap.shape)
    dst = nc.dram_tensor(name, shape, BF16, kind="Internal").ap()

    @p.block.gpsimd
    def _(g):
        rows = shape[0]
        inner = int(np.prod(shape[1:])) if len(shape) > 1 else 1
        step = max(1, min(rows, (1 << 21) // max(inner, 1)))
        r = 0
        while r < rows:
            n = min(step, rows - r)
            p.dma(g, dst[r:r + n], src_ap[r:r + n], "g")
            r += n
    return dst


def load_wT(p, wbf_dram, sbuf_t, Min, Mout, transpose, col0=0, row0=0):
    @p.block.sync
    def _(sync):
        for ch in range(Min // 128):
            dst = sbuf_t[:, ch * Mout:(ch + 1) * Mout]
            if SIMSAFE:
                sync.wait_ge(p.sems["ld"], p.cnt["ld"])
            if transpose:
                i = sync.dma_start(
                    out=dst,
                    in_=wbf_dram[col0:col0 + Mout, row0 + 128 * ch:row0 + 128 * (ch + 1)],
                    transpose=True)
            else:
                i = sync.dma_start(
                    out=dst,
                    in_=wbf_dram[row0 + 128 * ch:row0 + 128 * (ch + 1), col0:col0 + Mout])
            p.inc(i, "ld", 16)



def emit_ln(p, c, xdram, T, xbf, tag):
    nc = p.nc
    nt = T // 512

    with ExitStack() as sc:
        E = sc.enter_context
        xs = [E(nc.sbuf_tensor(f"{tag}_xs{i}", [128, min(T, 2048)], F32)) for i in range(2)]
        SECL = min(T, 2048)
        nsecl = T // SECL
        sq = [E(nc.sbuf_tensor(f"{tag}_sq{i}", [128, 512], BF16)) for i in range(2)]
        mean_r = E(nc.sbuf_tensor(f"{tag}_mean", [1, T], F32))
        ex2_r = E(nc.sbuf_tensor(f"{tag}_ex2", [1, T], F32))
        rstd_r = E(nc.sbuf_tensor(f"{tag}_rstd", [1, T], F32))
        eps_c = E(nc.sbuf_tensor(f"{tag}_eps", [1, 1], F32))
        ps_sum = E(nc.psum_tensor(f"{tag}_pssum", [1, 512], F32))
        ps_sq = E(nc.psum_tensor(f"{tag}_pssq", [1, 512], F32))
        ps_bc = E(nc.psum_tensor(f"{tag}_psbc", [128, 512], F32))
        ps_bc2 = E(nc.psum_tensor(f"{tag}_psbc2", [128, 512], F32))

        ld0 = p.cnt["ld"]
        dve0 = p.cnt["dve"]

        @p.block.gpsimd
        def _(g):
            p.tick(g.memset(eps_c[:], 1e-5), "pool")

        @p.block.sync
        def _(sync):
            for ch in range(NCH):
                for se in range(nsecl):
                    j = ch * nsecl + se
                    if j >= 2:
                        p.wait(sync, "dve", dve0 + (j - 1))
                    p.dma(sync, xs[j % 2][:, :],
                          xdram[128 * ch:128 * (ch + 1), se * SECL:(se + 1) * SECL], "ld")

        @p.block.vector
        def _(v):
            for ch in range(NCH):
                for se in range(nsecl):
                    j = ch * nsecl + se
                    p.wait(v, "ld", ld0 + 16 * (j + 1))
                    p.tick(v.tensor_copy(
                        xbf[:, ch * T + se * SECL: ch * T + (se + 1) * SECL],
                        xs[j % 2][:, :]), "dve")

        p.barrier([p.block.scalar, p.block.tensor])

        actB0 = p.cnt["act"]
        peB0 = p.cnt["pe"]
        dveB0 = p.cnt["dve"]

        @p.block.scalar
        def _(s):
            for ti in range(nt):
                for ch in range(NCH):
                    j = ti * NCH + ch
                    if j >= 2:
                        p.wait(s, "pe", peB0 + 2 * (j - 1))
                    sl = slice(ch * T + 512 * ti, ch * T + 512 * (ti + 1))
                    p.tick(s.activation(out=sq[j % 2][:], in_=xbf[:, sl],
                                        func=AF.Square), "act")

        @p.block.tensor
        def _(t):
            for ti in range(nt):
                for ch in range(NCH):
                    j = ti * NCH + ch
                    if ch == 0:
                        p.wait(t, "dve", dveB0 + 2 * ti)
                    p.wait(t, "act", actB0 + j + 1)
                    sl = slice(ch * T + 512 * ti, ch * T + 512 * (ti + 1))
                    p.tick(t.matmul(ps_sum[:], c.ones_col[:], xbf[:, sl],
                                    start=(ch == 0), stop=(ch == NCH - 1)), "pe")
                    p.tick(t.matmul(ps_sq[:], c.ones_col[:], sq[j % 2][:],
                                    start=(ch == 0), stop=(ch == NCH - 1)), "pe")

        @p.block.vector
        def _(v):
            for ti in range(nt):
                p.wait(v, "pe", peB0 + 16 * (ti + 1))
                p.tick(v.tensor_copy(mean_r[:, 512 * ti:512 * (ti + 1)], ps_sum[:]), "dve")
                p.tick(v.tensor_copy(ex2_r[:, 512 * ti:512 * (ti + 1)], ps_sq[:]), "dve")

        p.barrier([p.block.vector])

        @p.block.vector
        def _(v):
            p.dep(v, v.tensor_scalar(out=mean_r[:, :], in0=mean_r[:, :],
                                     scalar1=1.0 / D, scalar2=None,
                                     op0=ALU.mult), "dve")
            p.dep(v, v.tensor_scalar(out=ex2_r[:, :], in0=ex2_r[:, :],
                                     scalar1=1.0 / D, scalar2=None,
                                     op0=ALU.mult), "dve")
            p.dep(v, v.tensor_mul(rstd_r[:, :], mean_r[:, :], mean_r[:, :]), "dve")
            p.tick(v.tensor_sub(rstd_r[:, :], ex2_r[:, :], rstd_r[:, :]), "dve")

        p.barrier([p.block.scalar])

        @p.block.scalar
        def _(s):
            p.tick(s.activation(out=rstd_r[:, :], in_=rstd_r[:, :], func=AF.Sqrt,
                                bias=eps_c[0:1, 0:1]), "act")

        p.barrier([p.block.vector])

        @p.block.vector
        def _(v):
            p.tick(v.reciprocal(rstd_r[:, :], rstd_r[:, :]), "dve")

        p.barrier([p.block.tensor])

        peD0 = p.cnt["pe"]
        dveD0 = p.cnt["dve"]

        @p.block.tensor
        def _(t):
            for ti in range(nt):
                p.wait(t, "dve", dveD0 + 2 * NCH * ti)
                sl = slice(512 * ti, 512 * (ti + 1))
                p.tick(t.matmul(ps_bc[:], c.ones_rowf[:], mean_r[:, sl],
                                start=True, stop=True), "pe")
                p.tick(t.matmul(ps_bc2[:], c.ones_rowf[:], rstd_r[:, sl],
                                start=True, stop=True), "pe")

        @p.block.vector
        def _(v):
            for ti in range(nt):
                p.wait(v, "pe", peD0 + 2 * (ti + 1))
                for ch in range(NCH):
                    sl = slice(ch * T + 512 * ti, ch * T + 512 * (ti + 1))
                    p.dep(v, v.tensor_sub(xbf[:, sl], xbf[:, sl], ps_bc[:]), "dve")
                    p.tick(v.tensor_mul(xbf[:, sl], xbf[:, sl], ps_bc2[:]), "dve")

        p.barrier()



def emit_rope_table(p, c, posrow_sb, freq_ap, T, dst, shift, tag):
    nc = p.nc
    nt = T // 512

    with ExitStack() as sc:
        E = sc.enter_context
        tmp = E(nc.sbuf_tensor(f"{tag}_tmp", [64, T], F32))
        msk = E(nc.sbuf_tensor(f"{tag}_msk", [64, T], F32))
        tmpi = E(nc.sbuf_tensor(f"{tag}_tmpi", [64, T], I32))
        ps = E(nc.psum_tensor(f"{tag}_ps", [64, 512], F32))

        pe0 = p.cnt["pe"]
        dve0 = p.cnt["dve"]

        @p.block.tensor
        def _(t):
            for ti in range(nt):
                p.wait(t, "dve", dve0 + ti)
                p.tick(t.matmul(ps[:], c.ones_rowf[0:1, 0:64],
                                posrow_sb[:, 512 * ti:512 * (ti + 1)],
                                start=True, stop=True), "pe")

        @p.block.vector
        def _(v):
            for ti in range(nt):
                p.wait(v, "pe", pe0 + ti + 1)
                p.tick(v.tensor_scalar(out=tmp[:, 512 * ti:512 * (ti + 1)], in0=ps[:],
                                       scalar1=freq_ap,
                                       scalar2=1.0 / TWO_PI,
                                       op0=ALU.mult, op1=ALU.mult), "dve")

        p.barrier([p.block.vector, p.block.scalar])

        @p.block.vector
        def _(v):
            if shift != 0.0:
                p.dep(v, v.tensor_scalar(out=tmp[:, :], in0=tmp[:, :],
                                         scalar1=shift, scalar2=None,
                                         op0=ALU.add), "dve")
            p.dep(v, v.tensor_copy(tmpi[:, :], tmp[:, :]), "dve")
            p.dep(v, v.tensor_sub(tmp[:, :], tmp[:, :], tmpi[:, :]), "dve")
            p.dep(v, v.tensor_scalar(out=tmp[:, :], in0=tmp[:, :], scalar1=TWO_PI,
                                     scalar2=None, op0=ALU.mult), "dve")
            p.dep(v, v.tensor_scalar(out=msk[:, :], in0=tmp[:, :],
                                     scalar1=float(math.pi), scalar2=None,
                                     op0=ALU.is_gt), "dve")
            p.dep(v, v.scalar_tensor_tensor(out=tmp[:, :], in0=msk[:, :],
                                            scalar=-TWO_PI, in1=tmp[:, :],
                                            op0=ALU.mult, op1=ALU.add), "dve")
            p.dep(v, v.tensor_scalar(out=msk[:, :], in0=tmp[:, :],
                                     scalar1=float(-math.pi), scalar2=None,
                                     op0=ALU.is_lt), "dve")
            p.tick(v.scalar_tensor_tensor(out=tmp[:, :], in0=msk[:, :],
                                          scalar=TWO_PI, in1=tmp[:, :],
                                          op0=ALU.mult, op1=ALU.add), "dve")
        p.barrier([p.block.scalar])

        @p.block.scalar
        def _(s):
            p.tick(s.activation(out=dst[:, :], in_=tmp[:, :], func=AF.Sin), "act")
        p.barrier()



def build_l0a(upto='full'):
    H = 16
    SCALE = 1.0 / 8.0
    nc = bass.Bass("TRN2", target_bir_lowering=False, debug=False)

    xf = nc.dram_tensor("xf", [D, S], F32, kind="ExternalInput").ap()
    qkv_w = nc.dram_tensor("qkv_w", [3 * D, D], F32, kind="ExternalInput").ap()
    out_w = nc.dram_tensor("out_w", [D, D], F32, kind="ExternalInput").ap()
    bias_p = nc.dram_tensor("bias_p", [1, 16], F32, kind="ExternalInput").ap()
    bias_a = nc.dram_tensor("bias_a", [1, 16], F32, kind="ExternalInput").ap()
    freqs = nc.dram_tensor("freqs", [64, 2], F32, kind="ExternalInput").ap()
    posrow = nc.dram_tensor("posrow", [1, S], F32, kind="ExternalInput").ap()
    dbank = nc.dram_tensor("dbank", [4, 128, 32, 512], BF16, kind="ExternalInput").ap()
    cdram = declare_consts(nc)

    xo = nc.dram_tensor("xo", [D, HALF], F32, kind="ExternalOutput").ap()
    dbg = nc.dram_tensor("dbg", [128, NCH * S], BF16, kind="ExternalOutput").ap() \
        if upto != 'full' else None

    qdram = nc.dram_tensor("qdram", [H, 64, HALF], BF16, kind="Internal").ap()
    kdram = nc.dram_tensor("kdram", [H, 64, S], BF16, kind="Internal").ap()
    vdram = nc.dram_tensor("vdram", [H, 128, 32, 64], BF16, kind="Internal").ap()

    with ExitStack() as stack:
        p = P(nc, stack)
        c = load_consts(p, cdram)

        qkvw_bf = emit_cast_weight(p, qkv_w, "qkvw_bf")
        outw_bf = emit_cast_weight(p, out_w, "outw_bf")

        pa_row = p.sb("pa_row", [1, 32], F32)
        freq_sb = p.sb("freq_sb", [64, 2], F32)
        AP_col = p.sb("AP_col", [128, 32], F32)
        NB_col = p.sb("NB_col", [128, 16], F32)

        @p.block.sync
        def _(sync):
            p.dma(sync, pa_row[:, 0:16], bias_p[:, :], "ld")
            p.dma(sync, pa_row[:, 16:32], bias_a[:, :], "ld")
            p.dma(sync, freq_sb[:], freqs[:, :], "ld")
        p.barrier()

        ps_b = p.psum("ps_b", [128, 32])

        @p.block.vector
        def _(v):
            p.dep(v, v.tensor_scalar(out=pa_row[:, :], in0=pa_row[:, :],
                                     scalar1=0.01, scalar2=None,
                                     op0=ALU.max), "dve")
            p.tick(v.tensor_scalar(out=pa_row[:, 0:16], in0=pa_row[:, 0:16],
                                   scalar1=-1.0, scalar2=None, op0=ALU.mult), "dve")
        p.barrier([p.block.tensor])

        @p.block.tensor
        def _(t):
            p.tick(t.matmul(ps_b[:], c.ones_rowf[:], pa_row[:, :],
                            start=True, stop=True), "pe")
        p.barrier([p.block.vector])

        @p.block.vector
        def _(v):
            p.tick(v.tensor_copy(AP_col[:, :], ps_b[:]), "dve")
        p.barrier()

        with ExitStack() as ph1:
            E = ph1.enter_context
            ctab = E(nc.sbuf_tensor("ctab", [64, S], BF16))
            stab = E(nc.sbuf_tensor("stab", [64, S], BF16))
            with ExitStack() as rts:
                posrow_sb = rts.enter_context(nc.sbuf_tensor("posrow_sb", [1, S], F32))

                @p.block.sync
                def _(sync):
                    p.dma(sync, posrow_sb[:], posrow[:, :], "ld")
                p.barrier()
                emit_rope_table(p, c, posrow_sb, freq_sb[0:64, 0:1], S, stab, 0.0, "rts")
                emit_rope_table(p, c, posrow_sb, freq_sb[0:64, 1:2], S, ctab, 0.25, "rtc")

            xbf = E(nc.sbuf_tensor("xbf", [128, NCH * S], BF16))
            emit_ln(p, c, xf, S, xbf, "ln")

            if upto == 'ln':
                @p.block.gpsimd
                def _(g):
                    p.dma(g, dbg[:, :], xbf[:, :], "st")
                    p.wait(g, "st", p.cnt["st"])
                return nc

            qkvwT = E(nc.sbuf_tensor("qkvwT", [128, NCH * 3 * D], BF16))
            load_wT(p, qkvw_bf, qkvwT, D, 3 * D, transpose=True)
            p.barrier()

            SEC = 2048
            qtmp = E(nc.sbuf_tensor("qtmp", [64, SEC], F32))
            qrot = E(nc.sbuf_tensor("qrot", [64, SEC], F32))
            qswap = E(nc.sbuf_tensor("qswap", [64, SEC], F32))
            qbfh = E(nc.sbuf_tensor("qbfh", [64, SEC], BF16))
            sqh = E(nc.sbuf_tensor("sqh", [64, SEC], BF16))
            nrm = E(nc.sbuf_tensor("nrm", [1, S], F32))
            m2 = E(nc.sbuf_tensor("m2", [1, 4], F32))
            vebf = [E(nc.sbuf_tensor(f"vebf{i}", [128, 512], BF16)) for i in range(2)]
            ps_p = E(nc.psum_tensor("ps_p", [64, 512], F32))
            ps_v = E(nc.psum_tensor("ps_v", [128, 64], F32))
            ps_n = E(nc.psum_tensor("ps_n", [1, 512], F32))
            ps_nb = E(nc.psum_tensor("ps_nb", [128, 1], F32))

            def proj_qk(h, wcol0, scale, out_dram, Tcols, m2slot):
                nsec = Tcols // SEC
                for sec in range(nsec):
                    col0 = sec * SEC
                    peP0 = p.cnt["pe"]
                    dveP0 = p.cnt["dve"]

                    @p.block.tensor
                    def _(t, col0=col0, peP0=peP0, dveP0=dveP0):
                        for ti in range(SEC // 512):
                            p.wait(t, "dve", dveP0 + ti)
                            for ch in range(NCH):
                                i = t.matmul(
                                    ps_p[:],
                                    qkvwT[:, ch * 3 * D + wcol0: ch * 3 * D + wcol0 + 64],
                                    xbf[:, ch * S + col0 + 512 * ti:
                                        ch * S + col0 + 512 * (ti + 1)],
                                    start=(ch == 0), stop=(ch == NCH - 1))
                            p.tick(i, "pe")

                    @p.block.vector
                    def _(v, peP0=peP0):
                        for ti in range(SEC // 512):
                            p.wait(v, "pe", peP0 + ti + 1)
                            sl = slice(512 * ti, 512 * (ti + 1))
                            p.tick(v.tensor_scalar(out=qtmp[:, sl], in0=ps_p[:],
                                                   scalar1=scale, scalar2=None,
                                                   op0=ALU.mult), "dve")

                    p.barrier([p.block.vector])

                    @p.block.gpsimd
                    def _(g):
                        p.wait(g, "dve", p.cnt["dve"])
                        p.dma(g, qswap[0:32, :], qtmp[32:64, :], "g")
                        p.dma(g, qswap[32:64, :], qtmp[0:32, :], "g")

                    @p.block.vector
                    def _(v, col0=col0):
                        p.wait(v, "g", p.cnt["g"])
                        cs = ctab[:, col0:col0 + SEC]
                        sn = stab[:, col0:col0 + SEC]
                        p.dep(v, v.tensor_mul(qrot[:, :], qtmp[:, :], cs), "dve")
                        p.dep(v, v.tensor_mul(qswap[:, :], qswap[:, :], sn), "dve")
                        p.dep(v, v.tensor_add(qrot[:, :], qrot[:, :], qswap[:, :]),
                              "dve")
                        p.tick(v.tensor_copy(qbfh[:, :], qrot[:, :]), "dve")
                    p.barrier([p.block.gpsimd, p.block.scalar])

                    @p.block.gpsimd
                    def _(g, col0=col0, h=h):
                        p.dma(g, out_dram[h, :, col0:col0 + SEC], qbfh[:, :], "g")

                    @p.block.scalar
                    def _(s):
                        p.tick(s.activation(out=sqh[:, :], in_=qbfh[:, :],
                                            func=AF.Square), "act")
                    p.barrier([p.block.tensor])

                    peN0 = p.cnt["pe"]
                    dveN0 = p.cnt["dve"]

                    @p.block.tensor
                    def _(t, peN0=peN0, dveN0=dveN0):
                        for ti in range(SEC // 512):
                            p.wait(t, "dve", dveN0 + ti)
                            p.tick(t.matmul(ps_n[:], c.ones_col[0:64, :],
                                            sqh[:, 512 * ti:512 * (ti + 1)],
                                            start=True, stop=True), "pe")

                    @p.block.vector
                    def _(v, col0=col0, peN0=peN0):
                        for ti in range(SEC // 512):
                            p.wait(v, "pe", peN0 + ti + 1)
                            p.tick(v.tensor_copy(
                                nrm[:, col0 + 512 * ti:col0 + 512 * (ti + 1)],
                                ps_n[:]), "dve")
                    p.barrier([p.block.vector, p.block.gpsimd])

                @p.block.vector
                def _(v):
                    p.tick(v.reduce_max(m2[0:1, m2slot:m2slot + 1],
                                        nrm[0:1, 0:Tcols], axis=AX.X), "dve")
                p.barrier([p.block.vector, p.block.scalar, p.block.tensor])

            def proj_v(h):
                wcol0 = 2 * D + h * 64
                for grp in range(4):
                    peV0 = p.cnt["pe"]
                    dveV0 = p.cnt["dve"]
                    g0 = p.cnt["g"]

                    @p.block.tensor
                    def _(t, grp=grp, peV0=peV0, dveV0=dveV0, g0=g0):
                        for j in range(8):
                            kb = grp * 8 + j
                            p.wait(t, "dve", dveV0 + j)
                            for ch in range(NCH):
                                i = t.matmul(
                                    ps_v[:],
                                    xbf[:, ch * S + kb * 128: ch * S + (kb + 1) * 128],
                                    qkvwT[:, ch * 3 * D + wcol0: ch * 3 * D + wcol0 + 64],
                                    start=(ch == 0), stop=(ch == NCH - 1))
                            p.tick(i, "pe")

                    @p.block.vector
                    def _(v, grp=grp, peV0=peV0, g0=g0):
                        for j in range(8):
                            p.wait(v, "pe", peV0 + j + 1)
                            if grp >= 2 and j == 0:
                                p.wait(v, "g", g0)
                            p.tick(v.tensor_copy(
                                vebf[grp % 2][:, 64 * j:64 * (j + 1)], ps_v[:]), "dve")

                    if upto != 'projv0':
                        @p.block.gpsimd
                        def _(g, grp=grp, h=h, dveV0=dveV0):
                            p.wait(g, "dve", dveV0 + 8)
                            p.dma(g, vdram[h][:, grp * 8:(grp + 1) * 8, :],
                                  vebf[grp % 2][:].rearrange("p (k v) -> p k v", v=64), "g")
                p.barrier()

            nheads = 1 if upto in ('projq', 'projk', 'projv', 'projv0') else H
            for h in range(nheads):
                proj_qk(h, 0 * D + h * 64, SCALE, qdram, HALF, 0)
                if upto == 'projq':
                    break
                proj_qk(h, 1 * D + h * 64, 1.0, kdram, S, 1)
                if upto == 'projk':
                    break
                proj_v(h)
                if upto in ('projv', 'projv0'):
                    break

                @p.block.vector
                def _(v):
                    p.tick(v.tensor_mul(m2[0:1, 2:3], m2[0:1, 0:1], m2[0:1, 1:2]),
                           "dve")
                p.barrier([p.block.scalar])

                @p.block.scalar
                def _(s):
                    p.tick(s.activation(out=m2[0:1, 3:4], in_=m2[0:1, 2:3],
                                        func=AF.Sqrt), "act")
                p.barrier([p.block.vector])

                @p.block.vector
                def _(v):
                    p.tick(v.tensor_scalar(out=m2[0:1, 3:4], in0=m2[0:1, 3:4],
                                           scalar1=-1.0, scalar2=None,
                                           op0=ALU.mult), "dve")
                p.barrier([p.block.tensor])

                @p.block.tensor
                def _(t):
                    p.tick(t.matmul(ps_nb[:], c.ones_rowf[:], m2[0:1, 3:4],
                                    start=True, stop=True), "pe")
                p.barrier([p.block.vector])

                @p.block.vector
                def _(v, h=h):
                    p.tick(v.tensor_copy(NB_col[:, h:h + 1], ps_nb[:]), "dve")
                p.barrier()

        if upto in ('proj', 'projq', 'projk', 'projv', 'projv0'):
            @p.block.gpsimd
            def _(g):
                for hh in range(4):
                    p.dma(g, dbg[0:64, hh * S:(hh + 1) * S], kdram[hh], "st")
                for hh in range(4):
                    p.dma(g, dbg[64:128, hh * S: hh * S + HALF], qdram[hh], "st")
                p.wait(g, "st", p.cnt["st"])
            return nc

        p.barrier()
        Y = p.sb("Y", [128, 8 * HALF], BF16)
        with ExitStack() as ph2:
            E = ph2.enter_context
            dband = E(nc.sbuf_tensor("dband", [128, 32 * 512], BF16))
            Ksb = E(nc.sbuf_tensor("Ksb", [64, S], BF16))
            Vsb = E(nc.sbuf_tensor("Vsb", [128, 32 * 64], BF16))
            qsb = E(nc.sbuf_tensor("qsb", [64, 512], BF16))
            Lt = [E(nc.sbuf_tensor(f"Lt{i}", [128, 512], F32)) for i in range(2)]
            sx = [E(nc.sbuf_tensor(f"sx{i}", [128, 512], F32)) for i in range(2)]
            et = [E(nc.sbuf_tensor(f"et{i}", [128, 512], BF16)) for i in range(2)]
            sx = [E(nc.sbuf_tensor(f"sxb{i}", [128, 512], F32)) for i in range(2)]
            rden = E(nc.sbuf_tensor("rden", [1, 512], F32))
            yv = E(nc.sbuf_tensor("yv", [64, 512], F32))
            rbc = E(nc.sbuf_tensor("rbc", [64, 512], F32))
            ybf_s = E(nc.sbuf_tensor("ybf_s", [64, 512], BF16))
            ps_s = [E(nc.psum_tensor(f"ps_s{i}", [128, 512], F32)) for i in range(2)]
            ps_y = E(nc.psum_tensor("ps_y", [64, 512], F32))
            ps_d = E(nc.psum_tensor("ps_d", [1, 512], F32))
            ps_rb = E(nc.psum_tensor("ps_rb", [64, 512], F32))

            njq = 1 if upto in ('attn1', 'attnA', 'attnB') else 4
            nhh = 1 if upto in ('attn1', 'attnA', 'attnB') else H
            for jq in range(njq):
                @p.block.sync
                def _(sync, jq=jq):
                    p.wait(sync, "act", p.cnt["act"])
                    p.dma(sync, dband[:], dbank[jq].rearrange("p k q -> p (k q)"), "ld")
                p.barrier([p.block.scalar, p.block.sync])

                for h in range(nhh):
                    ld_h0 = p.cnt["ld"]

                    @p.block.sync
                    def _(sync, h=h, jq=jq):
                        p.wait(sync, "pe", p.cnt["pe"])
                        p.dma(sync, Ksb[:, :], kdram[h], "ld")
                        p.dma(sync, Vsb[:], vdram[h].rearrange("p k v -> p (k v)"), "ld")
                        p.dma(sync, qsb[:, :], qdram[h, :, jq * 512:(jq + 1) * 512], "ld")

                    p.barrier([p.block.tensor, p.block.scalar, p.block.vector])

                    b_pe = p.cnt["pe"]
                    b_act = p.cnt["act"]
                    b_dve = p.cnt["dve"]

                    if upto == 'attnA':
                        @p.block.tensor
                        def _(t):
                            for kb in range(32):
                                p.tick(t.matmul(ps_s[kb % 2][:],
                                                Ksb[:, kb * 128:(kb + 1) * 128],
                                                qsb[:, :], start=True, stop=True), "pe")
                                if kb >= 1:
                                    p.wait(t, "dve", p.cnt["dve"] - 0)
                        p.barrier([p.block.vector])

                        @p.block.vector
                        def _(v):
                            p.tick(v.tensor_copy(sx[0][:], ps_s[1][:]), "dve")
                        p.barrier()
                        continue

                    if upto == 'attnB':
                        @p.block.tensor
                        def _(t, b_dve=b_dve):
                            for kb in range(32):
                                p.wait(t, "dve", b_dve + max(0, kb - 1))
                                p.tick(t.matmul(ps_s[kb % 2][:],
                                                Ksb[:, kb * 128:(kb + 1) * 128],
                                                qsb[:, :], start=True, stop=True), "pe")

                        @p.block.scalar
                        def _(s, h=h, b_dve=b_dve):
                            for kb in range(32):
                                p.wait(s, "dve", b_dve + max(0, kb - 1))
                                p.tick(s.activation(out=Lt[kb % 2][:],
                                                    in_=dband[:, kb * 512:(kb + 1) * 512],
                                                    func=AF.Ln,
                                                    scale=AP_col[:, 16 + h:17 + h],
                                                    bias=c.ones_colf[:, 0:1]), "act")

                        @p.block.vector
                        def _(v, h=h, b_pe=b_pe, b_act=b_act, b_dve=b_dve):
                            for kb in range(32):
                                p.wait(v, "pe", b_pe + kb + 1)
                                p.wait(v, "act", b_act + kb + 1)
                                p.tick(v.scalar_tensor_tensor(
                                    out=sx[kb % 2][:], in0=Lt[kb % 2][:],
                                    scalar=AP_col[:, h:h + 1], in1=ps_s[kb % 2][:],
                                    op0=ALU.mult, op1=ALU.add), "dve")
                        p.barrier()
                        continue

                    @p.block.tensor
                    def _(t, b_pe=b_pe, b_act=b_act, b_dve=b_dve):
                        for kb in range(32):
                            p.wait(t, "dve", b_dve + max(0, kb - 1))
                            p.tick(t.matmul(ps_s[kb % 2][:],
                                            Ksb[:, kb * 128:(kb + 1) * 128],
                                            qsb[:, :], start=True, stop=True), "pe")
                            if kb >= 1:
                                p.wait(t, "act", b_act + 2 * (kb - 1) + 2)
                                p.tick(t.matmul(ps_y[:], Vsb[:, (kb - 1) * 64:kb * 64],
                                                et[(kb - 1) % 2][:],
                                                start=(kb == 1), stop=False), "pe")
                                p.tick(t.matmul(ps_d[:], c.ones_col[:],
                                                et[(kb - 1) % 2][:],
                                                start=(kb == 1), stop=False), "pe")
                        p.wait(t, "act", b_act + 2 * 31 + 2)
                        p.tick(t.matmul(ps_y[:], Vsb[:, 31 * 64:32 * 64], et[31 % 2][:],
                                        start=False, stop=True), "pe")
                        p.tick(t.matmul(ps_d[:], c.ones_col[:], et[31 % 2][:],
                                        start=False, stop=True), "pe")

                    @p.block.scalar
                    def _(s, h=h, b_pe=b_pe, b_act=b_act, b_dve=b_dve):
                        for kb in range(32):
                            p.wait(s, "dve", b_dve + max(0, kb - 1))
                            p.tick(s.activation(out=Lt[kb % 2][:],
                                                in_=dband[:, kb * 512:(kb + 1) * 512],
                                                func=AF.Ln,
                                                scale=AP_col[:, 16 + h:17 + h],
                                                bias=c.ones_colf[:, 0:1]), "act")
                            p.wait(s, "dve", b_dve + kb + 1)
                            if kb >= 2:
                                p.wait(s, "pe", b_pe + 3 * kb - 2)
                            p.tick(s.activation(out=et[kb % 2][:], in_=sx[kb % 2][:],
                                                func=AF.Exp,
                                                bias=NB_col[:, h:h + 1]), "act")

                    @p.block.vector
                    def _(v, h=h, b_pe=b_pe, b_act=b_act, b_dve=b_dve):
                        for kb in range(32):
                            p.wait(v, "pe", b_pe + (1 if kb == 0 else 3 * kb - 1))
                            p.wait(v, "act", b_act + 2 * kb + 1)
                            p.tick(v.scalar_tensor_tensor(
                                out=sx[kb % 2][:], in0=Lt[kb % 2][:],
                                scalar=AP_col[:, h:h + 1], in1=ps_s[kb % 2][:],
                                op0=ALU.mult, op1=ALU.add), "dve")

                    @p.block.vector
                    def _(v):
                        p.wait(v, "pe", p.cnt["pe"])
                        p.tick(v.reciprocal(rden[:, :], ps_d[:]), "dve")
                    p.barrier([p.block.tensor])

                    @p.block.tensor
                    def _(t):
                        p.tick(t.matmul(ps_rb[:], c.ones_rowf[0:1, 0:64],
                                        rden[:, :], start=True, stop=True), "pe")
                    p.barrier([p.block.vector])

                    ycol = (h // 2) * HALF + jq * 512

                    @p.block.vector
                    def _(v, h=h, ycol=ycol):
                        p.dep(v, v.tensor_copy(rbc[:, :], ps_rb[:]), "dve")
                        p.dep(v, v.tensor_copy(yv[:, :], ps_y[:, :]), "dve")
                        if h % 2 == 0:
                            p.tick(v.tensor_mul(Y[0:64, ycol:ycol + 512],
                                                yv[:, :], rbc[:, :]), "dve")
                        else:
                            p.tick(v.tensor_mul(ybf_s[:, :], yv[:, :], rbc[:, :]),
                                   "dve")

                    if h % 2 == 1:
                        @p.block.gpsimd
                        def _(g, ycol=ycol):
                            p.wait(g, "dve", p.cnt["dve"])
                            p.dma(g, Y[64:128, ycol:ycol + 512], ybf_s[:, :], "g")
                    p.barrier([p.block.tensor, p.block.scalar, p.block.vector])
            p.barrier()

        if upto in ('attn', 'attn1', 'attnA', 'attnB'):
            @p.block.gpsimd
            def _(g):
                p.dma(g, dbg[:, 0:8 * HALF], Y[:, :], "st")
                p.wait(g, "st", p.cnt["st"])
            return nc

        with ExitStack() as ph3:
            E = ph3.enter_context
            outwT = E(nc.sbuf_tensor("outwT", [128, NCH * D], BF16))
            xown = E(nc.sbuf_tensor("xown", [128, HALF], F32))
            xout = E(nc.sbuf_tensor("xout", [128, HALF], F32))
            ps_o = E(nc.psum_tensor("ps_o", [128, 512], F32))

            load_wT(p, outw_bf, outwT, D, D, transpose=True)
            p.barrier()

            for od in range(NCH):
                ld_o0 = p.cnt["ld"]

                @p.block.sync
                def _(sync, od=od):
                    p.wait(sync, "st", p.cnt["st"])
                    p.dma(sync, xown[:, :], xf[128 * od:128 * (od + 1), 0:HALF], "ld")

                peO0 = p.cnt["pe"]
                dveO0 = p.cnt["dve"]

                @p.block.tensor
                def _(t, od=od, peO0=peO0, dveO0=dveO0):
                    for jq in range(4):
                        p.wait(t, "dve", dveO0 + jq)
                        for hv in range(NCH):
                            i = t.matmul(
                                ps_o[:],
                                outwT[:, hv * D + od * 128: hv * D + (od + 1) * 128],
                                Y[:, hv * HALF + jq * 512: hv * HALF + (jq + 1) * 512],
                                start=(hv == 0), stop=(hv == NCH - 1))
                        p.tick(i, "pe")

                @p.block.vector
                def _(v, od=od, peO0=peO0, ld_o0=ld_o0, b_st=p.cnt["st"]):
                    p.wait(v, "ld", ld_o0 + 16)
                    p.wait(v, "st", b_st)
                    for jq in range(4):
                        p.wait(v, "pe", peO0 + jq + 1)
                        sl = slice(512 * jq, 512 * (jq + 1))
                        p.tick(v.tensor_add(xout[:, sl], xown[:, sl], ps_o[:]), "dve")

                @p.block.gpsimd
                def _(g, od=od):
                    p.wait(g, "dve", p.cnt["dve"])
                    p.dma(g, xo[128 * od:128 * (od + 1), :], xout[:, :], "st")

            @p.block.gpsimd
            def _(g):
                p.wait(g, "st", p.cnt["st"])

    return nc


def l0a_in_maps(x, inputs):
    qkv_w = np.asarray(inputs["L0_attn_a_qkv_w"], np.float32)
    out_w = np.asarray(inputs["L0_attn_a_out_w"], np.float32)
    bias_p = np.asarray(inputs["L0_attn_a_bias_p"], np.float32).reshape(1, 16)
    bias_a = np.asarray(inputs["L0_attn_a_bias_a"], np.float32).reshape(1, 16)
    f = np.asarray(inputs["L0_attn_a_rope_freqs"], np.float32).reshape(32)
    freqs = np.stack([np.concatenate([-f, f]), np.concatenate([f, f])], axis=1)
    pos = np.arange(S, dtype=np.float32)

    in_maps = []
    for core in range(NCORES):
        b, h = core // 2, core % 2
        perm = core_perm(h)
        xfm = fm(x[b])[:, perm]
        posr = pos[perm][None, :]
        import ml_dtypes
        dfull = np.abs(posr[0][:, None] - posr[0][None, :HALF]).astype(np.float32)
        dbank = np.ascontiguousarray(
            dfull.reshape(32, 128, 4, 512).transpose(2, 1, 0, 3)
        ).astype(ml_dtypes.bfloat16)
        m = dict(CONST_ARRAYS)
        m.update(dict(xf=np.ascontiguousarray(xfm), qkv_w=qkv_w, out_w=out_w,
                      bias_p=bias_p, bias_a=bias_a, freqs=freqs,
                      posrow=np.ascontiguousarray(posr), dbank=dbank))
        in_maps.append(m)

    def assemble(results):
        xn = np.empty((B, S, D), np.float32)
        for core in range(NCORES):
            b, h = core // 2, core % 2
            xn[b, h * HALF:(h + 1) * HALF, :] = results[core]["xo"].T
        return xn

    return in_maps, assemble


def build_l0f():
    T = 2048
    PP = 512
    nc = bass.Bass("TRN2", target_bir_lowering=False, debug=False)

    xf = nc.dram_tensor("xf", [D, T], F32, kind="ExternalInput").ap()
    wnames = ["wf", "uf", "wi", "ui", "wc", "uc", "wo", "uo"]
    wd = {n: nc.dram_tensor(n, [D, D], F32, kind="ExternalInput").ap() for n in wnames}
    bia = nc.dram_tensor("bia", [128, 4 * NCH], F32, kind="ExternalInput").ap()
    inc_d = nc.dram_tensor("inc_d", [128, NCH], F32, kind="ExternalInput").ap()
    cdram = declare_consts(nc)
    xo = nc.dram_tensor("xo", [D, T], F32, kind="ExternalOutput").ap()

    with ExitStack() as stack:
        p = P(nc, stack)
        c = load_consts(p, cdram)

        wbf = {n: emit_cast_weight(p, wd[n], n + "_bf") for n in wnames}

        bia_sb = p.sb("bia_sb", [128, 4 * NCH], F32)
        inc_sb = p.sb("inc_sb", [128, NCH], F32)

        @p.block.sync
        def _(sync):
            p.dma(sync, bia_sb[:], bia[:, :], "ld")
            p.dma(sync, inc_sb[:], inc_d[:, :], "ld")
        p.barrier()

        xbf = p.sb("xbf", [128, NCH * T], BF16)
        emit_ln(p, c, xf, T, xbf, "ln")

        gw = {}
        for n in wnames[:6]:
            gw[n] = p.sb(n + "T", [128, NCH * D], BF16)
            load_wT(p, wbf[n], gw[n], D, D, transpose=True)
        p.barrier()

        cst = p.sb("cst", [128, NCH * PP], F32)
        cbf = [p.sb(f"cbf{i}", [128, NCH * PP], BF16) for i in range(2)]
        gtile = {n: p.sb("g_" + n, [128, PP], BF16) for n in ("f", "i", "g")}
        tmp2 = p.sb("tmp2", [128, PP], F32)
        xown = p.sb("xown", [128, T], F32)
        hout = p.sb("hout", [128, T], F32)
        ps_g = [p.psum(f"ps_g{i}", [128, PP]) for i in range(3)]
        ps_h = p.psum("ps_h", [128, PP])

        @p.block.vector
        def _(v):
            for ch in range(NCH):
                p.tick(v.tensor_scalar(out=cst[:, ch * PP:(ch + 1) * PP],
                                       in0=xbf[:, 0:PP], scalar1=0.0,
                                       scalar2=inc_sb[:, ch:ch + 1],
                                       op0=ALU.mult, op1=ALU.add), "dve")
        p.barrier([p.block.vector])

        @p.block.vector
        def _(v):
            for ch in range(NCH):
                p.tick(v.tensor_copy(cbf[0][:, ch * PP:(ch + 1) * PP],
                                     cst[:, ch * PP:(ch + 1) * PP]), "dve")
        p.barrier()

        GATES = (("f", "wf", "uf", 0, AF.Sigmoid), ("i", "wi", "ui", 1, AF.Sigmoid),
                 ("g", "wc", "uc", 2, AF.Tanh))

        for t in range(4):
            rpar, wpar = t % 2, (t + 1) % 2
            for od in range(NCH):
                b_pe = p.cnt["pe"]
                b_act = p.cnt["act"]
                b_dve = p.cnt["dve"]

                @p.block.tensor
                def _(tt, t=t, od=od, b_pe=b_pe, b_act=b_act, b_dve=b_dve,
                      rpar=rpar):
                    p.wait(tt, "act", b_act)
                    p.wait(tt, "dve", b_dve)
                    for gi, (nm, wn, un, pi, afn) in enumerate(GATES):
                        for ch in range(NCH):
                            tt.matmul(ps_g[pi][:],
                                      gw[wn][:, ch * D + od * 128: ch * D + (od + 1) * 128],
                                      xbf[:, ch * T + t * 512: ch * T + (t + 1) * 512],
                                      start=(ch == 0), stop=False)
                        for ch in range(NCH):
                            i = tt.matmul(ps_g[pi][:],
                                          gw[un][:, ch * D + od * 128: ch * D + (od + 1) * 128],
                                          cbf[rpar][:, ch * PP:(ch + 1) * PP],
                                          start=False, stop=(ch == NCH - 1))
                        p.tick(i, "pe")

                @p.block.scalar
                def _(s, od=od, b_pe=b_pe, b_dve=b_dve):
                    p.wait(s, "dve", b_dve)
                    for gi, (nm, wn, un, pi, afn) in enumerate(GATES):
                        p.wait(s, "pe", b_pe + gi + 1)
                        boff = pi * NCH + od
                        p.tick(s.activation(out=gtile[nm][:], in_=ps_g[pi][:],
                                            func=afn, bias=bia_sb[:, boff:boff + 1]),
                               "act")

                @p.block.vector
                def _(v, od=od, b_act=b_act, wpar=wpar):
                    p.wait(v, "act", b_act + 3)
                    sl = slice(od * PP, (od + 1) * PP)
                    p.dep(v, v.tensor_mul(cst[:, sl], gtile["f"][:], cst[:, sl]),
                          "dve")
                    p.dep(v, v.tensor_mul(tmp2[:, :], gtile["i"][:],
                                          gtile["g"][:]), "dve")
                    p.dep(v, v.tensor_add(cst[:, sl], cst[:, sl], tmp2[:, :]),
                          "dve")
                    p.tick(v.tensor_copy(cbf[wpar][:, sl], cst[:, sl]), "dve")
        p.barrier()

        load_wT(p, wbf["wo"], gw["wf"], D, D, transpose=True)
        load_wT(p, wbf["uo"], gw["uf"], D, D, transpose=True)
        gw["wo"], gw["uo"] = gw["wf"], gw["uf"]
        p.barrier()
        cfin = 4 % 2
        for od in range(NCH):
            ld0 = p.cnt["ld"]

            @p.block.sync
            def _(sync, od=od):
                p.wait(sync, "st", p.cnt["st"])
                p.dma(sync, xown[:, :], xf[od * 128:(od + 1) * 128, :], "ld")

            b_pe = p.cnt["pe"]
            b_act = p.cnt["act"]
            b_dve = p.cnt["dve"]

            @p.block.tensor
            def _(tt, od=od, b_pe=b_pe, b_act=b_act):
                for t in range(4):
                    p.wait(tt, "act", b_act + t)
                    for ch in range(NCH):
                        tt.matmul(ps_h[:],
                                  gw["wo"][:, ch * D + od * 128: ch * D + (od + 1) * 128],
                                  xbf[:, ch * T + t * 512: ch * T + (t + 1) * 512],
                                  start=(ch == 0), stop=False)
                    for ch in range(NCH):
                        i = tt.matmul(ps_h[:],
                                      gw["uo"][:, ch * D + od * 128: ch * D + (od + 1) * 128],
                                      cbf[cfin][:, ch * PP:(ch + 1) * PP],
                                      start=False, stop=(ch == NCH - 1))
                    p.tick(i, "pe")

            @p.block.scalar
            def _(s, od=od, b_pe=b_pe, b_st=p.cnt["st"]):
                p.wait(s, "st", b_st)
                for t in range(4):
                    p.wait(s, "pe", b_pe + t + 1)
                    boff = 3 * NCH + od
                    p.tick(s.activation(out=hout[:, t * 512:(t + 1) * 512],
                                        in_=ps_h[:], func=AF.Tanh,
                                        bias=bia_sb[:, boff:boff + 1]), "act")

            @p.block.vector
            def _(v, od=od, ld0=ld0):
                p.wait(v, "act", p.cnt["act"])
                p.wait(v, "ld", ld0 + 16)
                p.tick(v.tensor_add(hout[:, :], hout[:, :], xown[:, :]), "dve")

            @p.block.gpsimd
            def _(g, od=od):
                p.wait(g, "dve", p.cnt["dve"])
                p.dma(g, xo[od * 128:(od + 1) * 128, :], hout[:, :], "st")

        @p.block.gpsimd
        def _(g):
            p.wait(g, "st", p.cnt["st"])

    return nc


def l0f_in_maps(x, inputs):
    wnames = ["wf", "uf", "wi", "ui", "wc", "uc", "wo", "uo"]
    wmap = {n: np.ascontiguousarray(inputs["L0_ffn_" + n], dtype=np.float32)
            for n in wnames}
    bias = np.concatenate([
        np.asarray(inputs["L0_ffn_" + n], np.float32).reshape(NCH, 128).T
        for n in ("bf", "bi", "bc", "bo")], axis=1)
    inc = np.asarray(inputs["L0_ffn_in_c"], np.float32).reshape(NCH, 128).T

    in_maps = []
    for core in range(NCORES):
        sl = slice(core * 512, (core + 1) * 512)
        xfm = np.concatenate([fm(x[b])[:, sl] for b in range(B)], axis=1)
        m = dict(CONST_ARRAYS)
        m.update(wmap)
        m.update(dict(xf=np.ascontiguousarray(xfm), bia=bias, inc_d=inc))
        in_maps.append(m)

    def assemble(results):
        xn = np.empty((B, S, D), np.float32)
        for core in range(NCORES):
            sl = slice(core * 512, (core + 1) * 512)
            r = results[core]["xo"]
            for b in range(B):
                xn[b, sl, :] = r[:, b * 512:(b + 1) * 512].T
        return xn

    return in_maps, assemble


LAMBDA_INIT = 0.8 - 0.6 * math.exp(-0.3 * 12)


def build_l0b(upto='full'):
    H8 = 8
    nc = bass.Bass("TRN2", target_bir_lowering=False, debug=False)

    xf = nc.dram_tensor("xf", [D, S], F32, kind="ExternalInput").ap()
    Wq = nc.dram_tensor("Wq", [D, D], F32, kind="ExternalInput").ap()
    Wk = nc.dram_tensor("Wk", [D, D], F32, kind="ExternalInput").ap()
    Wv = nc.dram_tensor("Wv", [D, D], F32, kind="ExternalInput").ap()
    Wo = nc.dram_tensor("Wo", [D, D], F32, kind="ExternalInput").ap()
    lvecs = nc.dram_tensor("lvecs", [64, 4], F32, kind="ExternalInput").ap()
    subln = nc.dram_tensor("subln", [128, 1], F32, kind="ExternalInput").ap()
    cdram = declare_consts(nc)
    xo = nc.dram_tensor("xo", [D, HALF], F32, kind="ExternalOutput").ap()
    dbg = nc.dram_tensor("dbg", [128, 8 * HALF], BF16, kind="ExternalOutput").ap() \
        if upto not in ('full',) else None

    qdram = nc.dram_tensor("qdram", [16, 64, HALF], BF16, kind="Internal").ap()
    kdram = nc.dram_tensor("kdram", [16, 64, S], BF16, kind="Internal").ap()
    vdram = nc.dram_tensor("vdram", [H8, 128, 32, 128], BF16, kind="Internal").ap()

    with ExitStack() as stack:
        p = P(nc, stack)
        c = load_consts(p, cdram)

        wbf = {}
        for nm, ap_ in (("wq", Wq), ("wk", Wk), ("wv", Wv), ("wo", Wo)):
            wbf[nm] = emit_cast_weight(p, ap_, nm + "_bf")

        lv_sb = p.sb("lv_sb", [64, 4], F32)
        subln_sb = p.sb("subln_sb", [128, 1], F32)
        NB_col = p.sb("NB_col", [128, 16], F32)
        nlam_bc = p.sb("nlam_bc", [128, 1], F32)
        lscr = p.sb("lscr", [64, 2], F32)
        lrow = p.sb("lrow", [1, 8], F32)
        ps_dot = p.psum("ps_dot", [1, 2])
        ps_nb = p.psum("ps_nb", [128, 1])

        @p.block.sync
        def _(sync):
            p.dma(sync, lv_sb[:], lvecs[:, :], "ld")
            p.dma(sync, subln_sb[:], subln[:, :], "ld")
        p.barrier()

        @p.block.vector
        def _(v):
            v.tensor_mul(lscr[:, 0:1], lv_sb[:, 0:1], lv_sb[:, 1:2])
            p.tick(v.tensor_mul(lscr[:, 1:2], lv_sb[:, 2:3], lv_sb[:, 3:4]), "dve")
        p.barrier([p.block.tensor])

        @p.block.tensor
        def _(t):
            p.tick(t.matmul(ps_dot[:], c.ones_colf[0:64, :], lscr[:, :],
                            start=True, stop=True), "pe")
        p.barrier([p.block.scalar])

        @p.block.scalar
        def _(s):
            p.tick(s.activation(out=lrow[0:1, 0:2], in_=ps_dot[:], func=AF.Exp), "act")
        p.barrier([p.block.vector])

        @p.block.vector
        def _(v):
            p.dep(v, v.tensor_sub(lrow[0:1, 2:3], lrow[0:1, 1:2], lrow[0:1, 0:1]),
                  "dve")
            p.tick(v.tensor_scalar(out=lrow[0:1, 2:3], in0=lrow[0:1, 2:3],
                                   scalar1=-LAMBDA_INIT, scalar2=None,
                                   op0=ALU.add), "dve")
        p.barrier([p.block.tensor])

        @p.block.tensor
        def _(t):
            p.tick(t.matmul(ps_nb[:], c.ones_rowf[:], lrow[0:1, 2:3],
                            start=True, stop=True), "pe")
        p.barrier([p.block.vector])

        @p.block.vector
        def _(v):
            p.tick(v.tensor_copy(nlam_bc[:, :], ps_nb[:]), "dve")
        p.barrier()

        with ExitStack() as ph1:
            E = ph1.enter_context
            xbf = E(nc.sbuf_tensor("xbf", [128, NCH * S], BF16))
            emit_ln(p, c, xf, S, xbf, "ln")

            wqT = E(nc.sbuf_tensor("wqT", [128, NCH * D], BF16))
            wkT = E(nc.sbuf_tensor("wkT", [128, NCH * D], BF16))
            wvT = E(nc.sbuf_tensor("wvT", [128, NCH * D], BF16))
            load_wT(p, wbf["wq"], wqT, D, D, transpose=False)
            load_wT(p, wbf["wk"], wkT, D, D, transpose=False)
            load_wT(p, wbf["wv"], wvT, D, D, transpose=False)
            p.barrier()

            qbfh = E(nc.sbuf_tensor("qbfh", [64, S], BF16))
            sqh = E(nc.sbuf_tensor("sqh", [64, S], BF16))
            nrm = E(nc.sbuf_tensor("nrm", [1, S], F32))
            m2 = E(nc.sbuf_tensor("m2", [1, 4], F32))
            vebf = [E(nc.sbuf_tensor(f"vebf{i}", [128, 512], BF16)) for i in range(2)]
            ps_p = E(nc.psum_tensor("ps_p", [64, 512], F32))
            ps_v = E(nc.psum_tensor("ps_v", [128, 128], F32))
            ps_n = E(nc.psum_tensor("ps_n", [1, 512], F32))

            def proj_qk16(h16, wT, scale, out_dram, Tcols, m2slot):
                nt = Tcols // 512
                peP0 = p.cnt["pe"]
                dveP0 = p.cnt["dve"]
                actP0 = p.cnt["act"]

                @p.block.tensor
                def _(t, peP0=peP0, dveP0=dveP0, h16=h16):
                    for ti in range(nt):
                        p.wait(t, "dve", dveP0 + ti)
                        for ch in range(NCH):
                            i = t.matmul(
                                ps_p[:],
                                wT[:, ch * D + h16 * 64: ch * D + h16 * 64 + 64],
                                xbf[:, ch * S + 512 * ti: ch * S + 512 * (ti + 1)],
                                start=(ch == 0), stop=(ch == NCH - 1))
                        p.tick(i, "pe")

                @p.block.vector
                def _(v, peP0=peP0, actP0=actP0):
                    for ti in range(nt):
                        p.wait(v, "pe", peP0 + ti + 1)
                        if ti >= 1:
                            p.wait(v, "act", actP0 + ti)
                        sl = slice(512 * ti, 512 * (ti + 1))
                        p.tick(v.tensor_scalar(out=qbfh[:, sl], in0=ps_p[:],
                                               scalar1=scale, scalar2=None,
                                               op0=ALU.mult), "dve")

                @p.block.scalar
                def _(s, dveP0=dveP0):
                    for ti in range(nt):
                        p.wait(s, "dve", dveP0 + ti + 1)
                        sl = slice(512 * ti, 512 * (ti + 1))
                        p.tick(s.activation(out=sqh[:, sl], in_=qbfh[:, sl],
                                            func=AF.Square), "act")

                peN0 = p.cnt["pe"]
                dveN0 = p.cnt["dve"]

                @p.block.tensor
                def _(t, peN0=peN0, dveN0=dveN0, actP0=actP0):
                    for ti in range(nt):
                        p.wait(t, "dve", dveN0 + ti)
                        p.wait(t, "act", actP0 + ti + 1)
                        p.tick(t.matmul(ps_n[:], c.ones_col[0:64, :],
                                        sqh[:, 512 * ti:512 * (ti + 1)],
                                        start=True, stop=True), "pe")

                @p.block.vector
                def _(v, peN0=peN0):
                    for ti in range(nt):
                        p.wait(v, "pe", peN0 + ti + 1)
                        p.tick(v.tensor_copy(nrm[:, 512 * ti:512 * (ti + 1)],
                                             ps_n[:]), "dve")

                @p.block.gpsimd
                def _(g, h16=h16):
                    p.wait(g, "dve", p.cnt["dve"])
                    p.dma(g, out_dram[h16, :, 0:Tcols], qbfh[:, 0:Tcols], "g")
                p.barrier([p.block.vector])

                @p.block.vector
                def _(v, m2slot=m2slot):
                    p.tick(v.reduce_max(m2[0:1, m2slot:m2slot + 1],
                                        nrm[0:1, 0:Tcols], axis=AX.X), "dve")
                p.barrier()

            def proj_v8(h8):
                for grp in range(8):
                    peV0 = p.cnt["pe"]
                    dveV0 = p.cnt["dve"]
                    g0 = p.cnt["g"]

                    @p.block.tensor
                    def _(t, grp=grp, peV0=peV0, dveV0=dveV0, h8=h8):
                        for j in range(4):
                            kb = grp * 4 + j
                            p.wait(t, "dve", dveV0 + j)
                            for ch in range(NCH):
                                i = t.matmul(
                                    ps_v[:],
                                    xbf[:, ch * S + kb * 128: ch * S + (kb + 1) * 128],
                                    wvT[:, ch * D + h8 * 128: ch * D + (h8 + 1) * 128],
                                    start=(ch == 0), stop=(ch == NCH - 1))
                            p.tick(i, "pe")

                    @p.block.vector
                    def _(v, grp=grp, peV0=peV0, g0=g0):
                        for j in range(4):
                            p.wait(v, "pe", peV0 + j + 1)
                            if grp >= 2 and j == 0:
                                p.wait(v, "g", g0)
                            p.tick(v.tensor_copy(
                                vebf[grp % 2][:, 128 * j:128 * (j + 1)], ps_v[:]),
                                "dve")

                    @p.block.gpsimd
                    def _(g, grp=grp, h8=h8, dveV0=dveV0):
                        p.wait(g, "dve", dveV0 + 4)
                        p.dma(g, vdram[h8][:, grp * 4:(grp + 1) * 4, :],
                              vebf[grp % 2][:].rearrange("p (k v) -> p k v", v=128),
                              "g")
                p.barrier()

            for h16 in range(16):
                proj_qk16(h16, wqT, 0.125, qdram, HALF, 0)
                proj_qk16(h16, wkT, 1.0, kdram, S, 1)

                @p.block.vector
                def _(v):
                    p.tick(v.tensor_mul(m2[0:1, 2:3], m2[0:1, 0:1], m2[0:1, 1:2]),
                           "dve")
                p.barrier([p.block.scalar])

                @p.block.scalar
                def _(s):
                    p.tick(s.activation(out=m2[0:1, 3:4], in_=m2[0:1, 2:3],
                                        func=AF.Sqrt), "act")
                p.barrier([p.block.vector])

                @p.block.vector
                def _(v):
                    p.tick(v.tensor_scalar(out=m2[0:1, 3:4], in0=m2[0:1, 3:4],
                                           scalar1=-1.0, scalar2=None,
                                           op0=ALU.mult), "dve")
                p.barrier([p.block.tensor])

                @p.block.tensor
                def _(t):
                    p.tick(t.matmul(ps_nb[:], c.ones_rowf[:], m2[0:1, 3:4],
                                    start=True, stop=True), "pe")
                p.barrier([p.block.vector])

                @p.block.vector
                def _(v, h16=h16):
                    p.tick(v.tensor_copy(NB_col[:, h16:h16 + 1], ps_nb[:]), "dve")
                p.barrier()

            for h8 in range(H8):
                proj_v8(h8)

        if upto == 'proj':
            @p.block.gpsimd
            def _(g):
                for hh in range(4):
                    p.dma(g, dbg[0:64, hh * S:(hh + 1) * S], kdram[hh], "st")
                p.dma(g, dbg[64:128, 4096:4096 + 2048], qdram[0], "st")
                p.wait(g, "st", p.cnt["st"])
            return nc

        Y = p.sb("Y", [128, H8 * HALF], BF16)
        with ExitStack() as ph2:
            E = ph2.enter_context
            Ksb = E(nc.sbuf_tensor("Ksb", [64, S], BF16))
            Vsb = E(nc.sbuf_tensor("Vsb", [128, 32 * 128], BF16))
            qsb = E(nc.sbuf_tensor("qsb", [64, 512], BF16))
            et = [E(nc.sbuf_tensor(f"et{i}", [128, 512], BF16)) for i in range(2)]
            sx = [E(nc.sbuf_tensor(f"sxb{i}", [128, 512], F32)) for i in range(2)]
            rden = E(nc.sbuf_tensor("rden", [1, 512], F32))
            rbc = E(nc.sbuf_tensor("rbc", [64 * 2, 512], F32))
            ysb = [E(nc.sbuf_tensor(f"ysb{i}", [128, 512], F32)) for i in range(2)]
            af = E(nc.sbuf_tensor("af", [128, 512], F32))
            af2 = E(nc.sbuf_tensor("af2", [128, 512], BF16))
            rms_r = E(nc.sbuf_tensor("rms_r", [1, 512], F32))
            ps_s = [E(nc.psum_tensor(f"ps_s{i}", [128, 512], F32)) for i in range(2)]
            ps_y = E(nc.psum_tensor("ps_y", [128, 512], F32))
            ps_d = E(nc.psum_tensor("ps_d", [1, 512], F32))
            ps_rb = E(nc.psum_tensor("ps_rb", [128, 512], F32))

            def subpass(h16, jq, slot):
                ld0 = p.cnt["ld"]

                @p.block.sync
                def _(sync, h16=h16, jq=jq):
                    p.wait(sync, "pe", p.cnt["pe"])
                    p.dma(sync, Ksb[:, :], kdram[h16], "ld")
                    p.dma(sync, qsb[:, :], qdram[h16, :, jq * 512:(jq + 1) * 512],
                          "ld")
                p.barrier([p.block.tensor, p.block.scalar, p.block.vector])

                b_pe = p.cnt["pe"]
                b_act = p.cnt["act"]
                b_dve = p.cnt["dve"]

                @p.block.tensor
                def _(t, b_pe=b_pe, b_act=b_act, b_dve=b_dve):
                    for kb in range(32):
                        p.wait(t, "dve", b_dve + max(0, kb - 1))
                        p.tick(t.matmul(ps_s[kb % 2][:],
                                        Ksb[:, kb * 128:(kb + 1) * 128],
                                        qsb[:, :], start=True, stop=True), "pe")
                        if kb >= 1:
                            p.wait(t, "act", b_act + kb)
                            p.tick(t.matmul(ps_y[:], Vsb[:, (kb - 1) * 128:kb * 128],
                                            et[(kb - 1) % 2][:],
                                            start=(kb == 1), stop=False), "pe")
                            p.tick(t.matmul(ps_d[:], c.ones_col[:],
                                            et[(kb - 1) % 2][:],
                                            start=(kb == 1), stop=False), "pe")
                    p.wait(t, "act", b_act + 32)
                    p.tick(t.matmul(ps_y[:], Vsb[:, 31 * 128:32 * 128], et[31 % 2][:],
                                    start=False, stop=True), "pe")
                    p.tick(t.matmul(ps_d[:], c.ones_col[:], et[31 % 2][:],
                                    start=False, stop=True), "pe")

                @p.block.vector
                def _(v, b_pe=b_pe, b_act=b_act):
                    for kb in range(32):
                        p.wait(v, "pe", b_pe + (1 if kb == 0 else 3 * kb - 1))
                        if kb >= 2:
                            p.wait(v, "act", b_act + kb - 1)
                        p.tick(v.tensor_copy(sx[kb % 2][:], ps_s[kb % 2][:]), "dve")

                @p.block.scalar
                def _(s, h16=h16, b_pe=b_pe, b_dve=b_dve):
                    for kb in range(32):
                        p.wait(s, "dve", b_dve + kb + 1)
                        if kb >= 2:
                            p.wait(s, "pe", b_pe + 3 * kb - 2)
                        p.tick(s.activation(out=et[kb % 2][:], in_=sx[kb % 2][:],
                                            func=AF.Exp,
                                            bias=NB_col[:, h16:h16 + 1]), "act")
                p.barrier()

                @p.block.vector
                def _(v):
                    p.wait(v, "pe", p.cnt["pe"])
                    p.tick(v.reciprocal(rden[:, :], ps_d[:]), "dve")
                p.barrier([p.block.tensor])

                @p.block.tensor
                def _(t):
                    p.tick(t.matmul(ps_rb[:], c.ones_rowf[:], rden[:, :],
                                    start=True, stop=True), "pe")
                p.barrier([p.block.vector])

                @p.block.vector
                def _(v, slot=slot):
                    p.dep(v, v.tensor_copy(rbc[0:128, :], ps_rb[:]), "dve")
                    p.tick(v.tensor_mul(ysb[slot][:], ps_y[:], rbc[0:128, :]), "dve")
                p.barrier([p.block.tensor, p.block.scalar])

            njq = 1 if upto == 'attn1' else 4
            nh8 = 1 if upto in ('attn1', 'attnjq', 'attnc2') else H8
            for jq in range(njq):
                for h8 in range(nh8):
                    @p.block.sync
                    def _(sync, h8=h8):
                        p.wait(sync, "pe", p.cnt["pe"])
                        p.dma(sync, Vsb[:], vdram[h8].rearrange("p k v -> p (k v)"),
                              "ld")
                    subpass(2 * h8, jq, 0)
                    subpass(2 * h8 + 1, jq, 1)
                    if upto == 'attn1':
                        @p.block.gpsimd
                        def _(g):
                            p.wait(g, "dve", p.cnt["dve"])
                            p.dma(g, dbg[:, 0:512], ysb[0][:], "st")
                            p.dma(g, dbg[:, 512:1024], ysb[1][:], "st")
                            p.wait(g, "st", p.cnt["st"])
                        return nc

                    b_dve = p.cnt["dve"]

                    @p.block.vector
                    def _(v, b_dve=b_dve):
                        p.wait(v, "dve", b_dve)
                        p.tick(v.scalar_tensor_tensor(out=af[:, :], in0=ysb[1][:],
                                                      scalar=nlam_bc[:, 0:1],
                                                      in1=ysb[0][:],
                                                      op0=ALU.mult, op1=ALU.add),
                               "dve")
                    p.barrier([p.block.scalar])

                    if upto == 'attnc2':
                        pass
                    if upto == 'attnc':
                        @p.block.vector
                        def _(v):
                            p.tick(v.tensor_copy(
                                rbc[0:128, 0:8],
                                nlam_bc[:, 0:1].to_broadcast([128, 8])), "dve")

                        @p.block.gpsimd
                        def _(g):
                            p.wait(g, "dve", p.cnt["dve"])
                            p.dma(g, dbg[:, 0:512], ysb[0][:], "st")
                            p.dma(g, dbg[:, 512:1024], ysb[1][:], "st")
                            p.dma(g, dbg[:, 1024:1536], af[:, :], "st")
                            p.dma(g, dbg[:, 1536:1544], rbc[0:128, 0:8], "st")
                            p.wait(g, "st", p.cnt["st"])
                        return nc

                    @p.block.scalar
                    def _(s):
                        p.tick(s.activation(out=af2[:, :], in_=af[:, :],
                                            func=AF.Square), "act")
                    p.barrier([p.block.tensor])

                    @p.block.tensor
                    def _(t):
                        p.tick(t.matmul(ps_d[:], c.ones_col[:], af2[:, :],
                                        start=True, stop=True), "pe")
                    p.barrier([p.block.vector])

                    @p.block.vector
                    def _(v):
                        p.tick(v.tensor_scalar(out=rms_r[:, :], in0=ps_d[:],
                                               scalar1=1.0 / 128.0, scalar2=1e-5,
                                               op0=ALU.mult, op1=ALU.add), "dve")
                    p.barrier([p.block.scalar])

                    if upto == 'attnc2':
                        @p.block.vector
                        def _(v):
                            p.tick(v.tensor_copy(ysb[1][0:1, 0:512], rms_r[:, :]),
                                   "dve")
                        p.barrier([p.block.scalar])

                    @p.block.scalar
                    def _(s):
                        p.tick(s.activation(out=rms_r[:, :], in_=rms_r[:, :],
                                            func=AF.Sqrt), "act")
                    p.barrier([p.block.vector])

                    if upto == 'attnc2':
                        @p.block.vector
                        def _(v):
                            p.tick(v.tensor_copy(rden[0:1, 0:512],
                                                 rms_r[:, :]), "dve")
                        p.barrier([p.block.vector])

                    @p.block.vector
                    def _(v):
                        p.tick(v.reciprocal(rms_r[:, :], rms_r[:, :]), "dve")
                    p.barrier([p.block.tensor])

                    @p.block.tensor
                    def _(t):
                        p.tick(t.matmul(ps_rb[:], c.ones_rowf[:], rms_r[:, :],
                                        start=True, stop=True), "pe")
                    p.barrier([p.block.vector])

                    @p.block.vector
                    def _(v, h8=h8, jq=jq):
                        p.dep(v, v.tensor_copy(rbc[0:128, :], ps_rb[:]), "dve")
                        p.dep(v, v.tensor_mul(af[:, :], af[:, :], rbc[0:128, :]),
                              "dve")
                        ycol = h8 * HALF + jq * 512
                        p.tick(v.tensor_scalar(out=Y[:, ycol:ycol + 512],
                                               in0=af[:, :],
                                               scalar1=subln_sb[:, 0:1],
                                               scalar2=float(1.0 - LAMBDA_INIT),
                                               op0=ALU.mult, op1=ALU.mult), "dve")
                    p.barrier()

                    if upto == 'attnc2':
                        @p.block.vector
                        def _(v):
                            p.tick(v.tensor_copy(ysb[0][0:1, :], rms_r[:, :]), "dve")

                        @p.block.gpsimd
                        def _(g):
                            p.wait(g, "dve", p.cnt["dve"])
                            p.dma(g, dbg[:, 0:512], af[:, :], "st")
                            p.dma(g, dbg[:, 512:1024], Y[:, 0:512], "st")
                            p.dma(g, dbg[0:1, 1024:1536], ysb[0][0:1, :], "st")
                            p.dma(g, dbg[:, 1536:2048], rbc[0:128, :], "st")
                            p.dma(g, dbg[0:1, 2048:2560], ysb[1][0:1, 0:512], "st")
                            p.dma(g, dbg[0:1, 2560:3072], rden[0:1, 0:512], "st")
                            p.dma(g, dbg[:, 3072:3584], af2[:, :], "st")
                            p.wait(g, "st", p.cnt["st"])
                        return nc

        if upto in ('attn', 'attnjq'):
            @p.block.gpsimd
            def _(g):
                p.dma(g, dbg[:, :], Y[:, :], "st")
                p.wait(g, "st", p.cnt["st"])
            return nc

        p.barrier()
        with ExitStack() as ph3:
            E = ph3.enter_context
            woT = E(nc.sbuf_tensor("woT", [128, NCH * D], BF16))
            xown = E(nc.sbuf_tensor("xown2", [128, HALF], F32))
            xout = E(nc.sbuf_tensor("xout2", [128, HALF], F32))
            ps_o = E(nc.psum_tensor("ps_o", [128, 512], F32))

            load_wT(p, wbf["wo"], woT, D, D, transpose=False)
            p.barrier()

            for od in range(NCH):
                ld_o0 = p.cnt["ld"]

                @p.block.sync
                def _(sync, od=od):
                    p.wait(sync, "st", p.cnt["st"])
                    p.dma(sync, xown[:, :], xf[128 * od:128 * (od + 1), 0:HALF], "ld")

                peO0 = p.cnt["pe"]
                dveO0 = p.cnt["dve"]

                @p.block.tensor
                def _(t, od=od, peO0=peO0, dveO0=dveO0):
                    for jq in range(4):
                        p.wait(t, "dve", dveO0 + jq)
                        for hv in range(NCH):
                            i = t.matmul(
                                ps_o[:],
                                woT[:, hv * D + od * 128: hv * D + (od + 1) * 128],
                                Y[:, hv * HALF + jq * 512: hv * HALF + (jq + 1) * 512],
                                start=(hv == 0), stop=(hv == NCH - 1))
                        p.tick(i, "pe")

                @p.block.vector
                def _(v, od=od, peO0=peO0, ld_o0=ld_o0, b_st=p.cnt["st"]):
                    p.wait(v, "ld", ld_o0 + 16)
                    p.wait(v, "st", b_st)
                    for jq in range(4):
                        p.wait(v, "pe", peO0 + jq + 1)
                        sl = slice(512 * jq, 512 * (jq + 1))
                        p.tick(v.tensor_add(xout[:, sl], xown[:, sl], ps_o[:]), "dve")

                @p.block.gpsimd
                def _(g, od=od):
                    p.wait(g, "dve", p.cnt["dve"])
                    p.dma(g, xo[128 * od:128 * (od + 1), :], xout[:, :], "st")
                p.barrier()

            @p.block.gpsimd
            def _(g):
                p.wait(g, "st", p.cnt["st"])

        if upto == 'post':
            @p.block.gpsimd
            def _(g):
                for hh in range(2):
                    p.dma(g, dbg[0:64, hh * S:(hh + 1) * S], kdram[hh], "st")
                p.dma(g, dbg[64:128, 0:2048], qdram[0], "st")
                p.wait(g, "st", p.cnt["st"])

        if upto == 'post2':
            @p.block.gpsimd
            def _(g):
                p.dma(g, dbg[:, :], Y[:, :], "st")
                p.wait(g, "st", p.cnt["st"])

    return nc


def l0b_in_maps(x, inputs):
    Wq = np.ascontiguousarray(inputs["L0_attn_b_Wq"], dtype=np.float32)
    Wk = np.ascontiguousarray(inputs["L0_attn_b_Wk"], dtype=np.float32)
    Wv = np.ascontiguousarray(inputs["L0_attn_b_Wv"], dtype=np.float32)
    Wo = np.ascontiguousarray(inputs["L0_attn_b_Wo"], dtype=np.float32)
    lvecs = np.stack([np.asarray(inputs["L0_attn_b_" + n], np.float32)
                      for n in ("lq1", "lk1", "lq2", "lk2")], axis=1)
    subln = np.asarray(inputs["L0_attn_b_subln_w"], np.float32).reshape(128, 1)

    in_maps = []
    for core in range(NCORES):
        b, h = core // 2, core % 2
        perm = core_perm(h)
        xfm = fm(x[b])[:, perm]
        m = dict(CONST_ARRAYS)
        m.update(dict(xf=np.ascontiguousarray(xfm), Wq=Wq, Wk=Wk, Wv=Wv, Wo=Wo,
                      lvecs=lvecs, subln=subln))
        in_maps.append(m)

    def assemble(results):
        xn = np.empty((B, S, D), np.float32)
        for core in range(NCORES):
            b, h = core // 2, core % 2
            xn[b, h * HALF:(h + 1) * HALF, :] = results[core]["xo"].T
        return xn

    return in_maps, assemble


def build_l1a():
    H = 16
    nc = bass.Bass("TRN2", target_bir_lowering=False, debug=False)

    xf = nc.dram_tensor("xf", [D, S], F32, kind="ExternalInput").ap()
    Wq = nc.dram_tensor("Wq", [D, D], F32, kind="ExternalInput").ap()
    Wk = nc.dram_tensor("Wk", [D, D], F32, kind="ExternalInput").ap()
    Wv = nc.dram_tensor("Wv", [D, D], F32, kind="ExternalInput").ap()
    Wo = nc.dram_tensor("Wo", [D, D], F32, kind="ExternalInput").ap()
    brows = nc.dram_tensor("brows", [1, 4 * D], F32, kind="ExternalInput").ap()
    cdram = declare_consts(nc)
    xo = nc.dram_tensor("xo", [D, HALF], F32, kind="ExternalOutput").ap()

    with ExitStack() as stack:
        p = P(nc, stack)
        c = load_consts(p, cdram)

        wbf = {}
        for nm, ap_ in (("wq", Wq), ("wk", Wk), ("wv", Wv), ("wo", Wo)):
            wbf[nm] = emit_cast_weight(p, ap_, nm + "_bf")

        brow_sb = p.sb("brow_sb", [1, 4 * D], BF16)
        ones_seg = p.sb("ones_seg", [1, 512], BF16)

        @p.block.gpsimd
        def _(g):
            p.dma(g, brow_sb[:], brows[:, :], "g")
            p.tick(g.memset(ones_seg[:], 1.0), "pool")
        p.barrier()

        qf_all = p.sb("qf_all", [128, 8 * HALF], BF16)
        kv_sb = p.sb("kv_sb", [128, 8 * 65], BF16)
        kvstage = p.sb("kvstage", [64, 65], BF16)
        Y = p.sb("Y", [128, 8 * HALF], BF16)

        scopeW = ExitStack()
        xbf = scopeW.enter_context(nc.sbuf_tensor("xbf", [128, NCH * S], BF16))
        emit_ln(p, c, xf, S, xbf, "ln")

        wqT = scopeW.enter_context(nc.sbuf_tensor("wqT", [128, NCH * D], BF16))
        wkT = scopeW.enter_context(nc.sbuf_tensor("wkT", [128, NCH * D], BF16))
        wvT = scopeW.enter_context(nc.sbuf_tensor("wvT", [128, NCH * D], BF16))
        load_wT(p, wbf["wq"], wqT, D, D, transpose=False)
        load_wT(p, wbf["wk"], wkT, D, D, transpose=False)
        load_wT(p, wbf["wv"], wvT, D, D, transpose=False)
        p.barrier()

        with ExitStack() as ph1:
            E = ph1.enter_context
            ps_p = E(nc.psum_tensor("ps_p", [64, 512], F32))
            el1 = E(nc.sbuf_tensor("el1", [64, 512], F32))
            el2 = E(nc.sbuf_tensor("el2", [64, 512], F32))
            el3 = E(nc.sbuf_tensor("el3", [64, 512], F32))

            for h in range(H):
                for jq in range(4):
                    b_pe = p.cnt["pe"]
                    b_dve = p.cnt["dve"]
                    b_act = p.cnt["act"]

                    @p.block.tensor
                    def _(t, h=h, jq=jq, b_dve=b_dve):
                        p.wait(t, "dve", b_dve)
                        for ch in range(NCH):
                            t.matmul(ps_p[:],
                                     wqT[:, ch * D + h * 64: ch * D + h * 64 + 64],
                                     xbf[:, ch * S + jq * 512: ch * S + (jq + 1) * 512],
                                     start=(ch == 0), stop=False)
                        p.tick(t.matmul(ps_p[:], brow_sb[0:1, 0 * D + h * 64:0 * D + h * 64 + 64],
                                        ones_seg[:, :], start=False, stop=True), "pe")

                    @p.block.vector
                    def _(v, b_pe=b_pe, b_act=b_act):
                        p.wait(v, "pe", b_pe + 1)
                        p.dep(v, v.tensor_scalar_min(el1[:, :], ps_p[:], 0.0), "dve")
                        p.tick(v.tensor_scalar_max(el3[:, :], ps_p[:], 0.0), "dve")

                    @p.block.scalar
                    def _(s, b_dve=b_dve):
                        p.wait(s, "dve", b_dve + 2)
                        p.tick(s.activation(out=el2[:, :], in_=el1[:, :],
                                            func=AF.Exp), "act")

                    @p.block.vector
                    def _(v, h=h, jq=jq, b_act=b_act):
                        p.wait(v, "act", b_act + 1)
                        col = (h // 2) * HALF + jq * 512
                        r0 = 64 * (h % 2)
                        p.tick(v.tensor_add(qf_all[r0:r0 + 64, col:col + 512]
                                            if r0 == 0 else el1[:, :],
                                            el2[:, :], el3[:, :]), "dve")
                    if h % 2 == 1:
                        @p.block.gpsimd
                        def _(g, h=h, jq=jq):
                            p.wait(g, "dve", p.cnt["dve"])
                            col = (h // 2) * HALF + jq * 512
                            p.dma(g, qf_all[64:128, col:col + 512], el1[:, :], "g")
                    p.barrier([p.block.vector, p.block.tensor, p.block.scalar])
            p.barrier()

        with ExitStack() as ph2:
            E = ph2.enter_context
            ps_k = E(nc.psum_tensor("ps_k", [128, 512], F32))
            ps_v = E(nc.psum_tensor("ps_v", [128, 512], F32))
            kf_t = [E(nc.sbuf_tensor(f"kf_t{i}", [128, D], BF16)) for i in range(2)]
            va_t = [E(nc.sbuf_tensor(f"va_t{i}", [128, H * 65], BF16)) for i in range(2)]
            ek1 = E(nc.sbuf_tensor("ek1", [128, 512], F32))
            ek2 = E(nc.sbuf_tensor("ek2", [128, 512], F32))
            ek3 = E(nc.sbuf_tensor("ek3", [128, 512], F32))
            ps_kv = [E(nc.psum_tensor(f"ps_kv{i}", [64, 65], F32)) for i in range(4)]
            kv_acc = E(nc.sbuf_tensor("kv_acc", [64, H * 65], F32))

            @p.block.gpsimd
            def _(g):
                p.tick(g.memset(kv_acc[:], 0.0), "pool")
            p.barrier([p.block.vector, p.block.tensor])

            @p.block.gpsimd
            def _(g):
                for i in range(2):
                    p.tick(g.memset(
                        va_t[i][:].rearrange("p (h a) -> p h a", a=65)[:, :, 64:65],
                        1.0), "pool")
            p.barrier()

            for kb in range(32):
                b_pe = p.cnt["pe"]
                b_dve = p.cnt["dve"]
                b_act = p.cnt["act"]

                @p.block.tensor
                def _(t, kb=kb, b_pe=b_pe, b_dve=b_dve, b_act=b_act):
                    p.wait(t, "dve", b_dve)
                    p.wait(t, "act", b_act)
                    for half in range(2):
                        for ch in range(NCH):
                            t.matmul(ps_k[:] if half == 0 else ps_v[:],
                                     xbf[:, ch * S + kb * 128: ch * S + (kb + 1) * 128],
                                     wkT[:, ch * D + half * 512: ch * D + (half + 1) * 512],
                                     start=(ch == 0), stop=False)
                        p.tick(t.matmul(ps_k[:] if half == 0 else ps_v[:],
                                        c.ones_row[:],
                                        brow_sb[0:1, 1 * D + half * 512:1 * D + (half + 1) * 512],
                                        start=False, stop=True), "pe")

                @p.block.vector
                def _(v, kb=kb, b_pe=b_pe, b_act=b_act):
                    for half in range(2):
                        ps = ps_k if half == 0 else ps_v
                        p.wait(v, "pe", b_pe + half + 1)
                        if kb >= 1:
                            p.wait(v, "pe", b_pe)
                        p.dep(v, v.tensor_scalar_min(ek1[:, :], ps[:], 0.0), "dve")
                        p.tick(v.tensor_scalar_max(ek3[:, :], ps[:], 0.0), "dve")
                        p.wait(v, "act", b_act + half + 1)
                        p.tick(v.tensor_add(
                            kf_t[kb % 2][:, half * 512:(half + 1) * 512],
                            ek2[:, :], ek3[:, :]), "dve")

                @p.block.scalar
                def _(s, b_dve=b_dve):
                    for half in range(2):
                        p.wait(s, "dve", b_dve + 3 * half + 2)
                        p.tick(s.activation(out=ek2[:, :], in_=ek1[:, :],
                                            func=AF.Exp), "act")

                p.barrier([p.block.tensor, p.block.vector, p.block.scalar])

                b_pe2 = p.cnt["pe"]
                b_dve2 = p.cnt["dve"]

                @p.block.tensor
                def _(t, kb=kb, b_pe2=b_pe2, b_dve2=b_dve2):
                    for half in range(2):
                        for ch in range(NCH):
                            t.matmul(ps_k[:] if half == 0 else ps_v[:],
                                     xbf[:, ch * S + kb * 128: ch * S + (kb + 1) * 128],
                                     wvT[:, ch * D + half * 512: ch * D + (half + 1) * 512],
                                     start=(ch == 0), stop=False)
                        p.tick(t.matmul(ps_k[:] if half == 0 else ps_v[:],
                                        c.ones_row[:],
                                        brow_sb[0:1, 2 * D + half * 512:2 * D + (half + 1) * 512],
                                        start=False, stop=True), "pe")

                @p.block.vector
                def _(v, kb=kb, b_pe2=b_pe2):
                    for half in range(2):
                        ps = ps_k if half == 0 else ps_v
                        p.wait(v, "pe", b_pe2 + half + 1)
                        for hh in range(8):
                            h = half * 8 + hh
                            p.tick(v.tensor_copy(
                                va_t[kb % 2][:, h * 65:h * 65 + 64],
                                ps[:, hh * 64:(hh + 1) * 64]), "dve")

                p.barrier([p.block.tensor])

                b_pe3 = p.cnt["pe"]
                b_dve3 = p.cnt["dve"]

                @p.block.tensor
                def _(t, kb=kb, b_dve3=b_dve3):
                    for h in range(H):
                        if h >= 4:
                            p.wait(t, "dve", b_dve3 + h - 3)
                        p.tick(t.matmul(
                            ps_kv[h % 4][:],
                            kf_t[kb % 2][:, h * 64:(h + 1) * 64],
                            va_t[kb % 2][:, h * 65:(h + 1) * 65],
                            start=True, stop=True), "pe")

                @p.block.vector
                def _(v, b_pe3=b_pe3):
                    for h in range(H):
                        p.wait(v, "pe", b_pe3 + h + 1)
                        p.tick(v.tensor_add(kv_acc[:, h * 65:(h + 1) * 65],
                                            kv_acc[:, h * 65:(h + 1) * 65],
                                            ps_kv[h % 4][:]), "dve")
                p.barrier([p.block.tensor, p.block.vector]) if kb == 31 else None

            for h in range(H):
                col = (h // 2) * 65
                if h % 2 == 0:
                    @p.block.vector
                    def _(v, h=h, col=col):
                        p.tick(v.tensor_copy(kv_sb[0:64, col:col + 65],
                                             kv_acc[:, h * 65:(h + 1) * 65]), "dve")
                else:
                    @p.block.vector
                    def _(v, h=h):
                        p.tick(v.tensor_copy(kvstage[:, :],
                                             kv_acc[:, h * 65:(h + 1) * 65]), "dve")

                    @p.block.gpsimd
                    def _(g, col=col):
                        p.wait(g, "dve", p.cnt["dve"])
                        p.dma(g, kv_sb[64:128, col:col + 65], kvstage[:, :], "g")
                    p.barrier([p.block.vector])
            p.barrier()

        p.barrier()
        scopeW.close()

        with ExitStack() as ph3:
            E = ph3.enter_context
            ps_y = E(nc.psum_tensor("ps_y", [64, 512], F32))
            ps_dn = E(nc.psum_tensor("ps_dn", [1, 512], F32))
            ps_rb = E(nc.psum_tensor("ps_rb", [64, 512], F32))
            rden = E(nc.sbuf_tensor("rden", [1, 512], F32))
            rbc = E(nc.sbuf_tensor("rbc", [64, 512], F32))
            yv = E(nc.sbuf_tensor("yv", [64, 512], F32))
            ybf_s = E(nc.sbuf_tensor("ybf_s", [64, 512], BF16))

            for h in range(H):
                for jq in range(4):
                    col = (h // 2) * HALF + jq * 512
                    r0 = 64 * (h % 2)
                    qsl = qf_all[r0:r0 + 64, col:col + 512]

                    b_pe = p.cnt["pe"]
                    b_dve = p.cnt["dve"]

                    kvr0 = 64 * (h % 2)
                    kvc = (h // 2) * 65

                    @p.block.tensor
                    def _(t, qsl=qsl, b_dve=b_dve, kvr0=kvr0, kvc=kvc):
                        p.wait(t, "dve", b_dve)
                        p.tick(t.matmul(ps_y[:],
                                        kv_sb[kvr0:kvr0 + 64, kvc:kvc + 64],
                                        qsl, start=True, stop=True), "pe")
                        p.tick(t.matmul(ps_dn[:],
                                        kv_sb[kvr0:kvr0 + 64, kvc + 64:kvc + 65],
                                        qsl, start=True, stop=True), "pe")

                    @p.block.vector
                    def _(v, b_pe=b_pe):
                        p.wait(v, "pe", b_pe + 2)
                        p.dep(v, v.tensor_scalar(out=rden[:, :], in0=ps_dn[:],
                                                 scalar1=1e-6, scalar2=None,
                                                 op0=ALU.add), "dve")
                        p.tick(v.reciprocal(rden[:, :], rden[:, :]), "dve")
                    p.barrier([p.block.tensor])

                    @p.block.tensor
                    def _(t):
                        p.tick(t.matmul(ps_rb[:], c.ones_rowf[0:1, 0:64],
                                        rden[:, :], start=True, stop=True), "pe")
                    p.barrier([p.block.vector])

                    @p.block.vector
                    def _(v, h=h, col=col, r0=r0):
                        p.dep(v, v.tensor_copy(rbc[:, :], ps_rb[:]), "dve")
                        if r0 == 0:
                            p.tick(v.tensor_mul(Y[0:64, col:col + 512],
                                                ps_y[:], rbc[:, :]), "dve")
                        else:
                            p.tick(v.tensor_mul(ybf_s[:, :], ps_y[:], rbc[:, :]),
                                   "dve")
                    if r0 != 0:
                        @p.block.gpsimd
                        def _(g, col=col):
                            p.wait(g, "dve", p.cnt["dve"])
                            p.dma(g, Y[64:128, col:col + 512], ybf_s[:, :], "g")
                    p.barrier([p.block.vector, p.block.tensor])
            p.barrier()

        with ExitStack() as ph4:
            E = ph4.enter_context
            woT = E(nc.sbuf_tensor("woT", [128, NCH * D], BF16))
            xown = E(nc.sbuf_tensor("xown3", [128, HALF], F32))
            xout = E(nc.sbuf_tensor("xout3", [128, HALF], F32))
            ps_o = E(nc.psum_tensor("ps_o", [128, 512], F32))

            load_wT(p, wbf["wo"], woT, D, D, transpose=False)
            p.barrier()

            for od in range(NCH):
                ld_o0 = p.cnt["ld"]

                @p.block.sync
                def _(sync, od=od):
                    p.wait(sync, "st", p.cnt["st"])
                    p.dma(sync, xown[:, :], xf[128 * od:128 * (od + 1), 0:HALF], "ld")

                peO0 = p.cnt["pe"]
                dveO0 = p.cnt["dve"]

                @p.block.tensor
                def _(t, od=od, peO0=peO0, dveO0=dveO0):
                    for jq in range(4):
                        p.wait(t, "dve", dveO0 + jq)
                        for hv in range(NCH):
                            t.matmul(ps_o[:],
                                     woT[:, hv * D + od * 128: hv * D + (od + 1) * 128],
                                     Y[:, hv * HALF + jq * 512: hv * HALF + (jq + 1) * 512],
                                     start=(hv == 0), stop=False)
                        p.tick(t.matmul(ps_o[:],
                                        brow_sb[0:1, 3 * D + od * 128:3 * D + (od + 1) * 128],
                                        ones_seg[:, :], start=False, stop=True), "pe")

                @p.block.vector
                def _(v, od=od, peO0=peO0, ld_o0=ld_o0, b_st=p.cnt["st"]):
                    p.wait(v, "ld", ld_o0 + 16)
                    p.wait(v, "st", b_st)
                    for jq in range(4):
                        p.wait(v, "pe", peO0 + jq + 1)
                        sl = slice(512 * jq, 512 * (jq + 1))
                        p.tick(v.tensor_add(xout[:, sl], xown[:, sl], ps_o[:]), "dve")

                @p.block.gpsimd
                def _(g, od=od):
                    p.wait(g, "dve", p.cnt["dve"])
                    p.dma(g, xo[128 * od:128 * (od + 1), :], xout[:, :], "st")

            @p.block.gpsimd
            def _(g):
                p.wait(g, "st", p.cnt["st"])

    return nc


def l1a_in_maps(x, inputs):
    Wq = np.ascontiguousarray(inputs["L1_attn_a_Wq"], dtype=np.float32).reshape(D, D)
    Wk = np.ascontiguousarray(inputs["L1_attn_a_Wk"], dtype=np.float32).reshape(D, D)
    Wv = np.ascontiguousarray(inputs["L1_attn_a_Wv"], dtype=np.float32).reshape(D, D)
    Wo = np.ascontiguousarray(inputs["L1_attn_a_Wo"], dtype=np.float32).reshape(D, D)
    brows = np.concatenate([
        np.asarray(inputs["L1_attn_a_bq"], np.float32).reshape(D),
        np.asarray(inputs["L1_attn_a_bk"], np.float32).reshape(D),
        np.asarray(inputs["L1_attn_a_bv"], np.float32).reshape(D),
        np.asarray(inputs["L1_attn_a_bo"], np.float32).reshape(D)])[None, :]

    in_maps = []
    for core in range(NCORES):
        b, h = core // 2, core % 2
        perm = core_perm(h)
        xfm = fm(x[b])[:, perm]
        m = dict(CONST_ARRAYS)
        m.update(dict(xf=np.ascontiguousarray(xfm), Wq=Wq, Wk=Wk, Wv=Wv, Wo=Wo,
                      brows=brows))
        in_maps.append(m)

    def assemble(results):
        xn = np.empty((B, S, D), np.float32)
        for core in range(NCORES):
            b, h = core // 2, core % 2
            xn[b, h * HALF:(h + 1) * HALF, :] = results[core]["xo"].T
        return xn

    return in_maps, assemble



def build_l1m():
    E_, I_, NI, MSH = 16, 1408, 11, 2816
    T = HALF
    TTS = 1024
    BIG = 1.0e30
    nc = bass.Bass("TRN2", target_bir_lowering=False, debug=False)

    xf = nc.dram_tensor("xf", [D, T], F32, kind="ExternalInput").ap()
    gwT = nc.dram_tensor("gwT", [D, 16], F32, kind="ExternalInput").ap()
    gbias = nc.dram_tensor("gbias", [1, 16], F32, kind="ExternalInput").ap()
    gpT = nc.dram_tensor("gpT", [E_, D, I_], F32, kind="ExternalInput").ap()
    upT = nc.dram_tensor("upT", [E_, D, I_], F32, kind="ExternalInput").ap()
    dpT = nc.dram_tensor("dpT", [E_, I_, D], F32, kind="ExternalInput").ap()
    sgwT = nc.dram_tensor("sgwT", [D, MSH], F32, kind="ExternalInput").ap()
    suwT = nc.dram_tensor("suwT", [D, MSH], F32, kind="ExternalInput").ap()
    sdwT = nc.dram_tensor("sdwT", [MSH, D], F32, kind="ExternalInput").ap()
    ramp4 = nc.dram_tensor("ramp4", [128, 4], F32, kind="ExternalInput").ap()
    ramp16 = nc.dram_tensor("ramp16", [128, 16], F32, kind="ExternalInput").ap()
    oh16 = nc.dram_tensor("oh16", [16, 16 * 128], F32, kind="ExternalInput").ap()
    ident = nc.dram_tensor("ident", [128, 128], F32, kind="ExternalInput").ap()
    cdram = declare_consts(nc)
    xo = nc.dram_tensor("xo", [D, T], F32, kind="ExternalOutput").ap()

    with ExitStack() as stack:
        p = P(nc, stack)
        c = load_consts(p, cdram)

        gw_bf = emit_cast_weight(p, gwT, "gw_bf")
        gp_bf = emit_cast_weight(p, gpT, "gp_bf")
        up_bf = emit_cast_weight(p, upT, "up_bf")
        dp_bf = emit_cast_weight(p, dpT, "dp_bf")
        sg_bf = emit_cast_weight(p, sgwT, "sg_bf")
        su_bf = emit_cast_weight(p, suwT, "su_bf")
        sd_bf = emit_cast_weight(p, sdwT, "sd_bf")

        r4 = p.sb("r4", [128, 4], F32)
        r16 = p.sb("r16", [128, 16], F32)
        oh = p.sb("oh", [16, 16 * 128], BF16)
        gb_row = p.sb("gb_row", [1, 16], F32)
        cwT = p.sb("cwT", [16, T], BF16)
        id_sb = p.sb("id_sb", [128, 128], BF16)

        @p.block.gpsimd
        def _(g):
            p.dma(g, r4[:], ramp4[:, :], "g")
            p.dma(g, r16[:], ramp16[:, :], "g")
            p.dma(g, gb_row[:], gbias[:, :], "g")
            p.dma(g, oh[:], oh16[:, :], "g")
            p.dma(g, id_sb[:], ident[:, :], "g")
        p.barrier()

        xbf = p.sb("xbf", [128, NCH * T], BF16)
        emit_ln(p, c, xf, T, xbf, "ln")

        with ExitStack() as gsc:
            Eg = gsc.enter_context
            gwT_sb = Eg(nc.sbuf_tensor("gwT_sb", [128, NCH * 16], BF16))
            lg = Eg(nc.sbuf_tensor("lg", [128, 256], F32))
            sfc = Eg(nc.sbuf_tensor("sfc", [128, 256], F32))
            eq = Eg(nc.sbuf_tensor("eq", [128, 256], F32))
            fsb = Eg(nc.sbuf_tensor("fsb", [128, 256], F32))
            ohx = Eg(nc.sbuf_tensor("ohx", [128, 256], F32))
            sel = Eg(nc.sbuf_tensor("sel", [128, 256], F32))
            m1 = Eg(nc.sbuf_tensor("m1", [128, 64], F32))
            rm = Eg(nc.sbuf_tensor("rm", [128, 64], F32))
            gsum = Eg(nc.sbuf_tensor("gsum", [128, 64], F32))
            gmask = Eg(nc.sbuf_tensor("gmask", [128, 64], F32))
            gb_bc = Eg(nc.sbuf_tensor("gb_bc", [128, 16], F32))
            cwbf = Eg(nc.sbuf_tensor("cwbf", [128, 256], BF16))
            ps_l = Eg(nc.psum_tensor("ps_l", [128, 16], F32))
            ps_gb = Eg(nc.psum_tensor("ps_gb", [128, 16], F32))

            load_wT(p, gw_bf, gwT_sb, D, 16, transpose=False)
            p.barrier()

            for tt in range(16):
                b_dve = p.cnt["dve"]
                b_pe = p.cnt["pe"]

                @p.block.tensor
                def _(t, tt=tt, b_dve=b_dve):
                    p.wait(t, "dve", b_dve)
                    for ch in range(NCH):
                        i = t.matmul(ps_l[:],
                                     xbf[:, ch * T + tt * 128: ch * T + (tt + 1) * 128],
                                     gwT_sb[:, ch * 16:(ch + 1) * 16],
                                     start=(ch == 0), stop=(ch == NCH - 1))
                    p.tick(i, "pe")

                @p.block.vector
                def _(v, tt=tt, b_pe=b_pe):
                    p.wait(v, "pe", b_pe + 1)
                    p.tick(v.tensor_copy(lg[:, tt * 16:(tt + 1) * 16], ps_l[:]),
                           "dve")

            @p.block.tensor
            def _(t):
                p.tick(t.matmul(ps_gb[:], c.ones_rowf[:], gb_row[:, :],
                                start=True, stop=True), "pe")
            p.barrier([p.block.scalar])

            @p.block.scalar
            def _(s):
                p.tick(s.activation(out=lg[:, :], in_=lg[:, :],
                                    func=AF.Sigmoid), "act")
            p.barrier([p.block.vector])

            def Vx(t_, inner):
                return t_[:, :].rearrange("p (a b) -> p a b", b=inner)

            @p.block.vector
            def _(v):
                p.dep(v, v.tensor_copy(gb_bc[:, :], ps_gb[:]), "dve")
                p.dep(v, v.tensor_tensor(
                    out=Vx(sfc, 16), in0=Vx(lg, 16),
                    in1=gb_bc[:, :].unsqueeze(1).to_broadcast([128, 16, 16]),
                    op=ALU.add), "dve")
                p.dep(v, v.reduce_max(Vx(m1, 1), Vx(sfc, 4), axis=AX.X), "dve")
                p.dep(v, v.tensor_tensor(out=Vx(eq, 4), in0=Vx(sfc, 4),
                                         in1=Vx(m1, 1).to_broadcast([128, 64, 4]),
                                         op=ALU.is_ge), "dve")
                p.dep(v, v.tensor_tensor(out=Vx(fsb, 4), in0=Vx(eq, 4),
                                         in1=r4[:, :].unsqueeze(1)
                                         .to_broadcast([128, 64, 4]),
                                         op=ALU.mult), "dve")
                p.dep(v, v.reduce_max(Vx(rm, 1), Vx(fsb, 4), axis=AX.X), "dve")
                p.dep(v, v.tensor_tensor(out=Vx(ohx, 4), in0=Vx(fsb, 4),
                                         in1=Vx(rm, 1).to_broadcast([128, 64, 4]),
                                         op=ALU.is_ge), "dve")
                p.dep(v, v.scalar_tensor_tensor(out=Vx(fsb, 4), in0=Vx(ohx, 4),
                                                scalar=-BIG, in1=Vx(sfc, 4),
                                                op0=ALU.mult, op1=ALU.add), "dve")
                p.dep(v, v.reduce_max(Vx(rm, 1), Vx(fsb, 4), axis=AX.X), "dve")
                p.dep(v, v.tensor_add(gsum[:, :], m1[:, :], rm[:, :]), "dve")

            @p.block.vector
            def _(v):
                def G4(t_):
                    return t_[:, 0:64].rearrange("p (a b) -> p a b", b=4)
                gm = m1
                gr = rm
                for itg in range(2):
                    p.dep(v, v.reduce_max(Vx(gm[:, 0:16], 1), G4(gsum),
                                          axis=AX.X), "dve")
                    p.dep(v, v.tensor_tensor(out=G4(eq), in0=G4(gsum),
                                             in1=Vx(gm[:, 0:16], 1)
                                             .to_broadcast([128, 16, 4]),
                                             op=ALU.is_ge), "dve")
                    p.dep(v, v.tensor_tensor(out=G4(fsb), in0=G4(eq),
                                             in1=r4[:, :].unsqueeze(1)
                                             .to_broadcast([128, 16, 4]),
                                             op=ALU.mult), "dve")
                    p.dep(v, v.reduce_max(Vx(gr[:, 0:16], 1), G4(fsb),
                                          axis=AX.X), "dve")
                    p.dep(v, v.tensor_tensor(out=G4(ohx), in0=G4(fsb),
                                             in1=Vx(gr[:, 0:16], 1)
                                             .to_broadcast([128, 16, 4]),
                                             op=ALU.is_ge), "dve")
                    if itg == 0:
                        p.dep(v, v.tensor_copy(gmask[:, :], ohx[:, 0:64]), "dve")
                        p.dep(v, v.scalar_tensor_tensor(out=G4(gsum),
                                                        in0=G4(ohx), scalar=-BIG,
                                                        in1=G4(gsum),
                                                        op0=ALU.mult,
                                                        op1=ALU.add), "dve")
                    else:
                        p.dep(v, v.tensor_add(gmask[:, :], gmask[:, :],
                                              ohx[:, 0:64]), "dve")

            @p.block.vector
            def _(v):
                p.dep(v, v.tensor_tensor(
                    out=sfc[:, :].rearrange("p (a b) -> p a b", b=4),
                    in0=sfc[:, :].rearrange("p (a b) -> p a b", b=4),
                    in1=Vx(gmask, 1).to_broadcast([128, 64, 4]),
                    op=ALU.mult), "dve")
                p.dep(v, v.memset(sel[:, :], 0.0), "dve")
                for it in range(4):
                    p.dep(v, v.reduce_max(Vx(m1[:, 0:16], 1), Vx(sfc, 16),
                                          axis=AX.X), "dve")
                    p.dep(v, v.tensor_tensor(out=Vx(eq, 16), in0=Vx(sfc, 16),
                                             in1=Vx(m1[:, 0:16], 1)
                                             .to_broadcast([128, 16, 16]),
                                             op=ALU.is_ge), "dve")
                    p.dep(v, v.tensor_tensor(out=Vx(fsb, 16), in0=Vx(eq, 16),
                                             in1=r16[:, :].unsqueeze(1)
                                             .to_broadcast([128, 16, 16]),
                                             op=ALU.mult), "dve")
                    p.dep(v, v.reduce_max(Vx(rm[:, 0:16], 1), Vx(fsb, 16),
                                          axis=AX.X), "dve")
                    p.dep(v, v.tensor_tensor(out=Vx(ohx, 16), in0=Vx(fsb, 16),
                                             in1=Vx(rm[:, 0:16], 1)
                                             .to_broadcast([128, 16, 16]),
                                             op=ALU.is_ge), "dve")
                    p.dep(v, v.tensor_add(sel[:, :], sel[:, :], ohx[:, :]), "dve")
                    p.dep(v, v.scalar_tensor_tensor(out=Vx(sfc, 16),
                                                    in0=Vx(ohx, 16), scalar=-BIG,
                                                    in1=Vx(sfc, 16),
                                                    op0=ALU.mult, op1=ALU.add),
                          "dve")
                p.dep(v, v.tensor_mul(sel[:, :], sel[:, :], lg[:, :]), "dve")
                p.dep(v, v.reduce_sum(Vx(m1[:, 0:16], 1), Vx(sel, 16),
                                      axis=AX.X), "dve")
                p.dep(v, v.tensor_scalar(out=m1[:, 0:16], in0=m1[:, 0:16],
                                         scalar1=1e-20, scalar2=None,
                                         op0=ALU.add), "dve")
                p.dep(v, v.reciprocal(m1[:, 0:16], m1[:, 0:16]), "dve")
                p.dep(v, v.tensor_tensor(out=Vx(sel, 16), in0=Vx(sel, 16),
                                         in1=Vx(m1[:, 0:16], 1)
                                         .to_broadcast([128, 16, 16]),
                                         op=ALU.mult), "dve")
                p.tick(v.tensor_copy(cwbf[:, :], sel[:, :]), "dve")
            p.barrier([p.block.tensor])

            ps_t = Eg(nc.psum_tensor("ps_t", [16, 128], F32))
            for tt in range(16):
                b_pe = p.cnt["pe"]
                b_dve = p.cnt["dve"]

                @p.block.tensor
                def _(t, tt=tt, b_dve=b_dve):
                    p.wait(t, "dve", b_dve)
                    p.tick(t.matmul(ps_t[:], cwbf[:, tt * 16:(tt + 1) * 16],
                                    id_sb[:], start=True, stop=True), "pe")

                @p.block.vector
                def _(v, tt=tt, b_pe=b_pe):
                    p.wait(v, "pe", b_pe + 1)
                    p.tick(v.tensor_copy(cwT[:, tt * 128:(tt + 1) * 128],
                                         ps_t[:]), "dve")
            p.barrier()

        y_acc = p.sb("y_acc", [128, NCH * TTS], F32)

        with ExitStack() as esc:
            Ee = esc.enter_context
            gp_sb = Ee(nc.sbuf_tensor("gp_sb", [128, NCH * I_], BF16))
            up_sb = Ee(nc.sbuf_tensor("up_sb", [128, NCH * I_], BF16))
            dp_sb = Ee(nc.sbuf_tensor("dp_sb", [128, NI * D], BF16))
            act_sb = Ee(nc.sbuf_tensor("act_sb", [128, NI * 512], BF16))
            silu_sb = [Ee(nc.sbuf_tensor(f"silu_sb{i}", [128, 512], BF16))
                       for i in range(2)]
            t1_sb = Ee(nc.sbuf_tensor("t1_sb", [128, 512], F32))
            cwbc_sb = Ee(nc.sbuf_tensor("cwbc_sb", [128, 512], BF16))
            xown = Ee(nc.sbuf_tensor("xown", [128, TTS], F32))
            xout = Ee(nc.sbuf_tensor("xout", [128, TTS], F32))
            ps_g = [Ee(nc.psum_tensor(f"ps_g{i}", [128, 512], F32))
                    for i in range(2)]
            ps_u = [Ee(nc.psum_tensor(f"ps_u{i}", [128, 512], F32))
                    for i in range(2)]
            ps_d = Ee(nc.psum_tensor("ps_d", [128, 512], F32))
            ps_cw = Ee(nc.psum_tensor("ps_cw", [128, 512], F32))

            pe_gu_done = 0
            pe_down_done = 0

            for sup in range(T // TTS):
                for e in range(E_ + 2):
                    @p.block.sync
                    def _(sync, e=e, pgu=pe_gu_done, pdn=pe_down_done):
                        sync.wait_ge(p.sems["pe"], pgu)
                        for ch in range(NCH):
                            if e < E_:
                                gsrc = gp_bf[e][128 * ch:128 * (ch + 1), :]
                                usrc = up_bf[e][128 * ch:128 * (ch + 1), :]
                            else:
                                hh = (e - E_) * I_
                                gsrc = sg_bf[128 * ch:128 * (ch + 1), hh:hh + I_]
                                usrc = su_bf[128 * ch:128 * (ch + 1), hh:hh + I_]
                            p.dma(sync, gp_sb[:, ch * I_:(ch + 1) * I_], gsrc, "ld")
                            p.dma(sync, up_sb[:, ch * I_:(ch + 1) * I_], usrc, "ld")
                        sync.wait_ge(p.sems["pe"], pdn)
                        for ic in range(NI):
                            if e < E_:
                                dsrc = dp_bf[e][128 * ic:128 * (ic + 1), :]
                            else:
                                r0 = (e - E_) * I_ + 128 * ic
                                dsrc = sd_bf[r0:r0 + 128, :]
                            p.dma(sync, dp_sb[:, ic * D:(ic + 1) * D], dsrc, "ld")
                    ld_after = p.cnt["ld"]

                    for sub in range(TTS // 512):
                        tcol = sup * TTS + sub * 512
                        is_exp = e < E_
                        P0 = p.cnt["pe"]
                        A0 = p.cnt["act"]
                        D0 = p.cnt["dve"]
                        cwk = 1 if is_exp else 0
                        nv = 2 if is_exp else 1

                        if is_exp:
                            @p.block.tensor
                            def _(t, e=e, tcol=tcol, D0=D0):
                                p.wait(t, "dve", D0)
                                p.tick(t.matmul(ps_cw[:],
                                                oh[:, e * 128:(e + 1) * 128],
                                                cwT[:, tcol:tcol + 512],
                                                start=True, stop=True), "pe")

                            @p.block.vector
                            def _(v, P0=P0):
                                p.wait(v, "pe", P0 + 1)
                                p.tick(v.tensor_copy(cwbc_sb[:, :], ps_cw[:]),
                                       "dve")
                        D1 = p.cnt["dve"]

                        for ic in range(NI):
                            bk = ic % 2

                            @p.block.tensor
                            def _(t, e=e, ic=ic, bk=bk, tcol=tcol, P0=P0,
                                  A0=A0, D1=D1, cwk=cwk, nv=nv,
                                  ld_after=ld_after):
                                if ic == 0:
                                    p.wait(t, "ld", ld_after - NI * 16)
                                p.wait(t, "act", A0 + max(0, ic - 1))
                                for ch in range(NCH):
                                    i = t.matmul(
                                        ps_g[bk][:],
                                        gp_sb[:, ch * I_ + ic * 128:
                                              ch * I_ + (ic + 1) * 128],
                                        xbf[:, ch * T + tcol: ch * T + tcol + 512],
                                        start=(ch == 0), stop=(ch == NCH - 1))
                                p.tick(i, "pe")
                                p.wait(t, "dve", D1 + nv * max(0, ic - 1))
                                for ch in range(NCH):
                                    i = t.matmul(
                                        ps_u[bk][:],
                                        up_sb[:, ch * I_ + ic * 128:
                                              ch * I_ + (ic + 1) * 128],
                                        xbf[:, ch * T + tcol: ch * T + tcol + 512],
                                        start=(ch == 0), stop=(ch == NCH - 1))
                                p.tick(i, "pe")

                            @p.block.scalar
                            def _(s, ic=ic, bk=bk, P0=P0, D1=D1, cwk=cwk,
                                  nv=nv):
                                p.wait(s, "pe", P0 + cwk + 2 * ic + 1)
                                p.wait(s, "dve", D1 + nv * max(0, ic - 1))
                                p.tick(s.activation(out=silu_sb[bk][:],
                                                    in_=ps_g[bk][:],
                                                    func=AF.Silu), "act")

                            @p.block.vector
                            def _(v, ic=ic, bk=bk, P0=P0, A0=A0, cwk=cwk,
                                  is_exp=is_exp):
                                p.wait(v, "act", A0 + ic + 1)
                                p.wait(v, "pe", P0 + cwk + 2 * ic + 2)
                                sl = slice(ic * 512, (ic + 1) * 512)
                                if is_exp:
                                    p.dep(v, v.scalar_tensor_tensor(
                                        out=t1_sb[:, :], in0=ps_u[bk][:],
                                        scalar=1.0, in1=silu_sb[bk][:],
                                        op0=ALU.mult, op1=ALU.mult), "dve")
                                    p.dep(v, v.tensor_mul(act_sb[:, sl],
                                                          t1_sb[:, :],
                                                          cwbc_sb[:, :]), "dve")
                                else:
                                    p.dep(v, v.scalar_tensor_tensor(
                                        out=act_sb[:, sl], in0=ps_u[bk][:],
                                        scalar=1.0, in1=silu_sb[bk][:],
                                        op0=ALU.mult, op1=ALU.mult), "dve")

                        V0 = p.cnt["dve"]
                        PD0 = p.cnt["pe"]

                        @p.block.tensor
                        def _(t, V0=V0, ld_after=ld_after):
                            p.wait(t, "ld", ld_after)
                            for Hch in range(NCH):
                                p.wait(t, "dve", V0 + Hch)
                                for ic in range(NI):
                                    i = t.matmul(
                                        ps_d[:],
                                        dp_sb[:, ic * D + Hch * 128:
                                              ic * D + (Hch + 1) * 128],
                                        act_sb[:, ic * 512:(ic + 1) * 512],
                                        start=(ic == 0), stop=(ic == NI - 1))
                                p.tick(i, "pe")

                        @p.block.vector
                        def _(v, e=e, sub=sub, PD0=PD0):
                            for Hch in range(NCH):
                                p.wait(v, "pe", PD0 + Hch + 1)
                                ysl = slice(Hch * TTS + sub * 512,
                                            Hch * TTS + (sub + 1) * 512)
                                if e == 0:
                                    p.dep(v, v.tensor_copy(y_acc[:, ysl],
                                                           ps_d[:]), "dve")
                                else:
                                    p.dep(v, v.tensor_add(y_acc[:, ysl],
                                                          y_acc[:, ysl],
                                                          ps_d[:]), "dve")
                        if sub == TTS // 512 - 1:
                            pe_gu_done = P0 + cwk + 2 * NI
                        pe_down_done = p.cnt["pe"]

                for od in range(NCH):
                    ld0 = p.cnt["ld"]

                    @p.block.sync
                    def _(sync, od=od, sup=sup):
                        p.wait(sync, "st", p.cnt["st"])
                        p.dma(sync, xown[:, :],
                              xf[128 * od:128 * (od + 1),
                                 sup * TTS:(sup + 1) * TTS], "ld")

                    @p.block.vector
                    def _(v, od=od, ld0=ld0, b_st=p.cnt["st"]):
                        p.wait(v, "ld", ld0 + 16)
                        p.wait(v, "st", b_st)
                        p.dep(v, v.tensor_add(xout[:, :], xown[:, :],
                                              y_acc[:, od * TTS:(od + 1) * TTS]),
                              "dve")

                    @p.block.gpsimd
                    def _(g, od=od, sup=sup):
                        p.wait(g, "dve", p.cnt["dve"])
                        p.dma(g, xo[128 * od:128 * (od + 1),
                                    sup * TTS:(sup + 1) * TTS], xout[:, :], "st")
                p.barrier()

            @p.block.gpsimd
            def _(g):
                p.wait(g, "st", p.cnt["st"])

    return nc


def l1m_in_maps(x, inputs, layer=1):
    pre = f"L{layer}_ffn_"
    gwT = np.ascontiguousarray(inputs[pre + "gate_w"].T, np.float32)
    gbias = np.asarray(inputs[pre + "gate_bias"], np.float32).reshape(1, 16)
    gpT = np.ascontiguousarray(inputs[pre + "gate_proj"].transpose(0, 2, 1),
                               np.float32)
    upT = np.ascontiguousarray(inputs[pre + "up_proj"].transpose(0, 2, 1),
                               np.float32)
    dpT = np.ascontiguousarray(inputs[pre + "down_proj"].transpose(0, 2, 1),
                               np.float32)
    sgwT = np.ascontiguousarray(inputs[pre + "shared_gate_w"].T, np.float32)
    suwT = np.ascontiguousarray(inputs[pre + "shared_up_w"].T, np.float32)
    sdwT = np.ascontiguousarray(inputs[pre + "shared_down_w"].T, np.float32)
    r4 = np.broadcast_to(np.arange(4, 0, -1, dtype=np.float32)[None, :],
                         (128, 4)).copy()
    r16 = np.broadcast_to(np.arange(16, 0, -1, dtype=np.float32)[None, :],
                          (128, 16)).copy()
    oh16 = np.zeros((16, 16 * 128), np.float32)
    for e in range(16):
        oh16[e, e * 128:(e + 1) * 128] = 1.0

    in_maps = []
    for core in range(NCORES):
        b, h = core // 2, core % 2
        xfm = fm(x[b])[:, h * HALF:(h + 1) * HALF]
        m = dict(CONST_ARRAYS)
        m.update(dict(xf=np.ascontiguousarray(xfm), gwT=gwT, gbias=gbias,
                      gpT=gpT, upT=upT, dpT=dpT, sgwT=sgwT, suwT=suwT,
                      sdwT=sdwT, ramp4=r4, ramp16=r16, oh16=oh16,
                      ident=np.eye(128, dtype=np.float32)))
        in_maps.append(m)

    def assemble(results):
        xn = np.empty((B, S, D), np.float32)
        for core in range(NCORES):
            b, h = core // 2, core % 2
            xn[b, h * HALF:(h + 1) * HALF, :] = results[core]["xo"].T
        return xn

    return in_maps, assemble


def build_l1b():
    H = 16
    nc = bass.Bass("TRN2", target_bir_lowering=False, debug=False)

    xf = nc.dram_tensor("xf", [D, S], F32, kind="ExternalInput").ap()
    WqT = nc.dram_tensor("WqT", [D, D], F32, kind="ExternalInput").ap()
    WkT = nc.dram_tensor("WkT", [D, D], F32, kind="ExternalInput").ap()
    WvT = nc.dram_tensor("WvT", [D, D], F32, kind="ExternalInput").ap()
    WoT = nc.dram_tensor("WoT", [D, D], F32, kind="ExternalInput").ap()
    brows = nc.dram_tensor("brows", [1, 4 * D], F32, kind="ExternalInput").ap()
    pdbank = nc.dram_tensor("pdbank", [H, 128, 5 * 512], F32,
                            kind="ExternalInput").ap()
    ctbank = nc.dram_tensor("ctbank", [128, H * 128], F32,
                            kind="ExternalInput").ap()
    cdram = declare_consts(nc)
    xo = nc.dram_tensor("xo", [D, HALF], F32, kind="ExternalOutput").ap()

    qdram = nc.dram_tensor("qdram", [H, 64, HALF], BF16, kind="Internal").ap()
    kdram = nc.dram_tensor("kdram", [H, 64, S], BF16, kind="Internal").ap()
    vdram = nc.dram_tensor("vdram", [H, 128, 32, 64], BF16, kind="Internal").ap()

    def kb_list(jq):
        return list(range(0, 4 * jq + 4)) + list(range(16, 32))

    with ExitStack() as stack:
        p = P(nc, stack)
        c = load_consts(p, cdram)

        wbf = {}
        for nm, ap_ in (("wq", WqT), ("wk", WkT), ("wv", WvT), ("wo", WoT)):
            wbf[nm] = emit_cast_weight(p, ap_, nm + "_bf")

        brow_sb = p.sb("brow_sb", [1, 4 * D], BF16)
        ones_seg = p.sb("ones_seg", [1, 2048], BF16)
        ct_sb = p.sb("ct_sb", [128, H * 128], F32)
        NB_col = p.sb("NB_col", [128, H], F32)

        @p.block.gpsimd
        def _(g):
            p.dma(g, brow_sb[:], brows[:, :], "g")
            p.dma(g, ct_sb[:], ctbank[:, :], "g")
            p.tick(g.memset(ones_seg[:], 1.0), "pool")
        p.barrier()

        with ExitStack() as ph1:
            E = ph1.enter_context
            xbf = E(nc.sbuf_tensor("xbf", [128, NCH * S], BF16))
            emit_ln(p, c, xf, S, xbf, "ln")

            wqT = E(nc.sbuf_tensor("wqT", [128, NCH * D], BF16))
            wkT = E(nc.sbuf_tensor("wkT", [128, NCH * D], BF16))
            wvT = E(nc.sbuf_tensor("wvT", [128, NCH * D], BF16))
            load_wT(p, wbf["wq"], wqT, D, D, transpose=False)
            load_wT(p, wbf["wk"], wkT, D, D, transpose=False)
            load_wT(p, wbf["wv"], wvT, D, D, transpose=False)
            p.barrier()

            SEC = 2048
            qbfh = E(nc.sbuf_tensor("qbfh", [64, SEC], BF16))
            sqh = E(nc.sbuf_tensor("sqh", [64, SEC], BF16))
            nrm = E(nc.sbuf_tensor("nrm", [1, S], F32))
            m2 = E(nc.sbuf_tensor("m2", [1, 4], F32))
            vebf = [E(nc.sbuf_tensor(f"vebf{i}", [128, 512], BF16)) for i in range(2)]
            ps_p = E(nc.psum_tensor("ps_p", [64, 512], F32))
            ps_v = E(nc.psum_tensor("ps_v", [128, 64], F32))
            ps_n = E(nc.psum_tensor("ps_n", [1, 512], F32))
            ps_nb = E(nc.psum_tensor("ps_nb", [128, 1], F32))

            def proj_qk(h, wT, bcol0, scale, out_dram, Tcols, m2slot):
                for sec in range(Tcols // SEC):
                    col0 = sec * SEC
                    peP0 = p.cnt["pe"]
                    dveP0 = p.cnt["dve"]

                    @p.block.tensor
                    def _(t, h=h, wT=wT, bcol0=bcol0, col0=col0, peP0=peP0,
                          dveP0=dveP0):
                        for ti in range(SEC // 512):
                            p.wait(t, "dve", dveP0 + ti)
                            for ch in range(NCH):
                                t.matmul(
                                    ps_p[:],
                                    wT[:, ch * D + h * 64: ch * D + h * 64 + 64],
                                    xbf[:, ch * S + col0 + 512 * ti:
                                        ch * S + col0 + 512 * (ti + 1)],
                                    start=(ch == 0), stop=False)
                            p.tick(t.matmul(
                                ps_p[:], brow_sb[0:1, bcol0 + h * 64:
                                                 bcol0 + h * 64 + 64],
                                ones_seg[0:1, 0:512], start=False, stop=True),
                                "pe")

                    @p.block.vector
                    def _(v, peP0=peP0):
                        for ti in range(SEC // 512):
                            p.wait(v, "pe", peP0 + ti + 1)
                            sl = slice(512 * ti, 512 * (ti + 1))
                            p.tick(v.tensor_scalar(out=qbfh[:, sl], in0=ps_p[:],
                                                   scalar1=scale, scalar2=None,
                                                   op0=ALU.mult), "dve")
                    p.barrier([p.block.vector, p.block.gpsimd, p.block.scalar])

                    @p.block.gpsimd
                    def _(g, col0=col0, h=h, out_dram=out_dram):
                        p.dma(g, out_dram[h, :, col0:col0 + SEC], qbfh[:, :], "g")

                    @p.block.scalar
                    def _(s):
                        p.tick(s.activation(out=sqh[:, :], in_=qbfh[:, :],
                                            func=AF.Square), "act")
                    p.barrier([p.block.tensor])

                    peN0 = p.cnt["pe"]
                    dveN0 = p.cnt["dve"]

                    @p.block.tensor
                    def _(t, peN0=peN0, dveN0=dveN0):
                        for ti in range(SEC // 512):
                            p.wait(t, "dve", dveN0 + ti)
                            p.tick(t.matmul(ps_n[:], c.ones_col[0:64, :],
                                            sqh[:, 512 * ti:512 * (ti + 1)],
                                            start=True, stop=True), "pe")

                    @p.block.vector
                    def _(v, col0=col0, peN0=peN0):
                        for ti in range(SEC // 512):
                            p.wait(v, "pe", peN0 + ti + 1)
                            p.tick(v.tensor_copy(
                                nrm[:, col0 + 512 * ti:col0 + 512 * (ti + 1)],
                                ps_n[:]), "dve")
                    p.barrier([p.block.vector, p.block.gpsimd])

                @p.block.vector
                def _(v, Tcols=Tcols, m2slot=m2slot):
                    p.tick(v.reduce_max(m2[0:1, m2slot:m2slot + 1],
                                        nrm[0:1, 0:Tcols], axis=AX.X), "dve")
                p.barrier([p.block.vector, p.block.scalar, p.block.tensor])

            def proj_v(h):
                for grp in range(4):
                    peV0 = p.cnt["pe"]
                    dveV0 = p.cnt["dve"]
                    g0 = p.cnt["g"]

                    @p.block.tensor
                    def _(t, h=h, grp=grp, peV0=peV0, dveV0=dveV0):
                        for j in range(8):
                            kb = grp * 8 + j
                            p.wait(t, "dve", dveV0 + j)
                            for ch in range(NCH):
                                t.matmul(
                                    ps_v[:],
                                    xbf[:, ch * S + kb * 128: ch * S + (kb + 1) * 128],
                                    wvT[:, ch * D + h * 64: ch * D + h * 64 + 64],
                                    start=(ch == 0), stop=False)
                            p.tick(t.matmul(
                                ps_v[:], c.ones_row[:],
                                brow_sb[0:1, 2 * D + h * 64: 2 * D + h * 64 + 64],
                                start=False, stop=True), "pe")

                    @p.block.vector
                    def _(v, grp=grp, peV0=peV0, g0=g0):
                        for j in range(8):
                            p.wait(v, "pe", peV0 + j + 1)
                            if grp >= 2 and j == 0:
                                p.wait(v, "g", g0)
                            p.tick(v.tensor_copy(
                                vebf[grp % 2][:, 64 * j:64 * (j + 1)], ps_v[:]),
                                "dve")

                    @p.block.gpsimd
                    def _(g, grp=grp, h=h, dveV0=dveV0):
                        p.wait(g, "dve", dveV0 + 8)
                        p.dma(g, vdram[h][:, grp * 8:(grp + 1) * 8, :],
                              vebf[grp % 2][:].rearrange("p (k v) -> p k v", v=64),
                              "g")
                p.barrier()

            for h in range(H):
                proj_qk(h, wqT, 0 * D, 0.125, qdram, HALF, 0)
                proj_qk(h, wkT, 1 * D, 1.0, kdram, S, 1)
                proj_v(h)

                @p.block.vector
                def _(v):
                    p.tick(v.tensor_mul(m2[0:1, 2:3], m2[0:1, 0:1], m2[0:1, 1:2]),
                           "dve")
                p.barrier([p.block.scalar])

                @p.block.scalar
                def _(s):
                    p.tick(s.activation(out=m2[0:1, 3:4], in_=m2[0:1, 2:3],
                                        func=AF.Sqrt), "act")
                p.barrier([p.block.vector])

                @p.block.vector
                def _(v):
                    p.tick(v.tensor_scalar(out=m2[0:1, 3:4], in0=m2[0:1, 3:4],
                                           scalar1=-1.0, scalar2=None,
                                           op0=ALU.mult), "dve")
                p.barrier([p.block.tensor])

                @p.block.tensor
                def _(t):
                    p.tick(t.matmul(ps_nb[:], c.ones_rowf[:], m2[0:1, 3:4],
                                    start=True, stop=True), "pe")
                p.barrier([p.block.vector])

                @p.block.vector
                def _(v, h=h):
                    p.tick(v.tensor_copy(NB_col[:, h:h + 1], ps_nb[:]), "dve")
                p.barrier()

        p.barrier()
        Y = p.sb("Y", [128, 8 * HALF], BF16)
        with ExitStack() as ph2:
            E = ph2.enter_context
            pdh = E(nc.sbuf_tensor("pdh", [128, 5 * 512], F32))
            Ksb = E(nc.sbuf_tensor("Ksb", [64, S], BF16))
            Vsb = E(nc.sbuf_tensor("Vsb", [128, 32 * 64], BF16))
            qsb = E(nc.sbuf_tensor("qsb", [64, 512], BF16))
            sx = [E(nc.sbuf_tensor(f"sxb{i}", [128, 512], F32)) for i in range(2)]
            et = [E(nc.sbuf_tensor(f"et{i}", [128, 512], BF16)) for i in range(2)]
            rden = E(nc.sbuf_tensor("rden", [1, 512], F32))
            yv = E(nc.sbuf_tensor("yv", [64, 512], F32))
            rbc = E(nc.sbuf_tensor("rbc", [64, 512], F32))
            ybf_s = E(nc.sbuf_tensor("ybf_s", [64, 512], BF16))
            ps_s = [E(nc.psum_tensor(f"ps_s{i}", [128, 512], F32)) for i in range(2)]
            ps_y = E(nc.psum_tensor("ps_y", [64, 512], F32))
            ps_d = E(nc.psum_tensor("ps_d", [1, 512], F32))
            ps_rb = E(nc.psum_tensor("ps_rb", [64, 512], F32))

            for h in range(H):
                @p.block.sync
                def _(sync, h=h):
                    p.wait(sync, "pe", p.cnt["pe"])
                    p.dma(sync, Ksb[:, :], kdram[h], "ld")
                    p.dma(sync, Vsb[:], vdram[h].rearrange("p k v -> p (k v)"), "ld")
                    p.dma(sync, pdh[:, :], pdbank[h], "ld")

                for jq in range(4):
                    kbs = kb_list(jq)
                    n = len(kbs)

                    @p.block.sync
                    def _(sync, h=h, jq=jq):
                        p.wait(sync, "pe", p.cnt["pe"])
                        p.dma(sync, qsb[:, :],
                              qdram[h, :, jq * 512:(jq + 1) * 512], "ld")
                    p.barrier([p.block.tensor, p.block.scalar, p.block.vector])

                    b_pe = p.cnt["pe"]
                    b_act = p.cnt["act"]
                    b_dve = p.cnt["dve"]

                    @p.block.tensor
                    def _(t, kbs=kbs, n=n, b_pe=b_pe, b_act=b_act, b_dve=b_dve):
                        for i, kb in enumerate(kbs):
                            p.wait(t, "dve", b_dve + max(0, i - 1))
                            p.tick(t.matmul(ps_s[i % 2][:],
                                            Ksb[:, kb * 128:(kb + 1) * 128],
                                            qsb[:, :], start=True, stop=True),
                                   "pe")
                            if i >= 1:
                                kbp = kbs[i - 1]
                                p.wait(t, "act", b_act + i)
                                p.tick(t.matmul(ps_y[:],
                                                Vsb[:, kbp * 64:(kbp + 1) * 64],
                                                et[(i - 1) % 2][:],
                                                start=(i == 1), stop=False), "pe")
                                p.tick(t.matmul(ps_d[:], c.ones_col[:],
                                                et[(i - 1) % 2][:],
                                                start=(i == 1), stop=False), "pe")
                        p.wait(t, "act", b_act + n)
                        kbp = kbs[n - 1]
                        p.tick(t.matmul(ps_y[:], Vsb[:, kbp * 64:(kbp + 1) * 64],
                                        et[(n - 1) % 2][:],
                                        start=False, stop=True), "pe")
                        p.tick(t.matmul(ps_d[:], c.ones_col[:], et[(n - 1) % 2][:],
                                        start=False, stop=True), "pe")

                    @p.block.scalar
                    def _(s, h=h, n=n, b_pe=b_pe, b_dve=b_dve):
                        for i in range(n):
                            p.wait(s, "dve", b_dve + i + 1)
                            if i >= 2:
                                p.wait(s, "pe", b_pe + 3 * i - 2)
                            p.tick(s.activation(out=et[i % 2][:], in_=sx[i % 2][:],
                                                func=AF.Exp,
                                                bias=NB_col[:, h:h + 1]), "act")

                    @p.block.vector
                    def _(v, h=h, jq=jq, kbs=kbs, b_pe=b_pe, b_dve=b_dve):
                        for i, kb in enumerate(kbs):
                            p.wait(v, "pe", b_pe + (1 if i == 0 else 3 * i - 1))
                            if 4 * jq <= kb < 4 * jq + 4:
                                slot = 1 + (kb - 4 * jq)
                            else:
                                slot = 0
                            ctc = h * 128 + jq * 32 + kb
                            p.tick(v.scalar_tensor_tensor(
                                out=sx[i % 2][:],
                                in0=pdh[:, slot * 512:(slot + 1) * 512],
                                scalar=ct_sb[:, ctc:ctc + 1],
                                in1=ps_s[i % 2][:],
                                op0=ALU.add, op1=ALU.add), "dve")

                    @p.block.vector
                    def _(v):
                        p.wait(v, "pe", p.cnt["pe"])
                        p.tick(v.reciprocal(rden[:, :], ps_d[:]), "dve")
                    p.barrier([p.block.tensor])

                    @p.block.tensor
                    def _(t):
                        p.tick(t.matmul(ps_rb[:], c.ones_rowf[0:1, 0:64],
                                        rden[:, :], start=True, stop=True), "pe")
                    p.barrier([p.block.vector])

                    ycol = (h // 2) * HALF + jq * 512

                    @p.block.vector
                    def _(v, h=h, ycol=ycol):
                        p.dep(v, v.tensor_copy(rbc[:, :], ps_rb[:]), "dve")
                        p.dep(v, v.tensor_copy(yv[:, :], ps_y[:, :]), "dve")
                        if h % 2 == 0:
                            p.tick(v.tensor_mul(Y[0:64, ycol:ycol + 512],
                                                yv[:, :], rbc[:, :]), "dve")
                        else:
                            p.tick(v.tensor_mul(ybf_s[:, :], yv[:, :], rbc[:, :]),
                                   "dve")

                    if h % 2 == 1:
                        @p.block.gpsimd
                        def _(g, ycol=ycol):
                            p.wait(g, "dve", p.cnt["dve"])
                            p.dma(g, Y[64:128, ycol:ycol + 512], ybf_s[:, :], "g")
                    p.barrier([p.block.tensor, p.block.scalar, p.block.vector])
            p.barrier()

        with ExitStack() as ph3:
            E = ph3.enter_context
            outwT = E(nc.sbuf_tensor("outwT", [128, NCH * D], BF16))
            xown = E(nc.sbuf_tensor("xown", [128, HALF], F32))
            xout = E(nc.sbuf_tensor("xout", [128, HALF], F32))
            ps_o = E(nc.psum_tensor("ps_o", [128, 512], F32))

            load_wT(p, wbf["wo"], outwT, D, D, transpose=False)
            p.barrier()

            for od in range(NCH):
                ld_o0 = p.cnt["ld"]

                @p.block.sync
                def _(sync, od=od):
                    p.wait(sync, "st", p.cnt["st"])
                    p.dma(sync, xown[:, :], xf[128 * od:128 * (od + 1), 0:HALF],
                          "ld")

                peO0 = p.cnt["pe"]
                dveO0 = p.cnt["dve"]

                @p.block.tensor
                def _(t, od=od, peO0=peO0, dveO0=dveO0):
                    for jq in range(4):
                        p.wait(t, "dve", dveO0 + jq)
                        for hv in range(NCH):
                            t.matmul(
                                ps_o[:],
                                outwT[:, hv * D + od * 128: hv * D + (od + 1) * 128],
                                Y[:, hv * HALF + jq * 512: hv * HALF + (jq + 1) * 512],
                                start=(hv == 0), stop=False)
                        p.tick(t.matmul(
                            ps_o[:],
                            brow_sb[0:1, 3 * D + od * 128: 3 * D + (od + 1) * 128],
                            ones_seg[0:1, 0:512], start=False, stop=True), "pe")

                @p.block.vector
                def _(v, od=od, peO0=peO0, ld_o0=ld_o0, b_st=p.cnt["st"]):
                    p.wait(v, "ld", ld_o0 + 16)
                    p.wait(v, "st", b_st)
                    for jq in range(4):
                        p.wait(v, "pe", peO0 + jq + 1)
                        sl = slice(512 * jq, 512 * (jq + 1))
                        p.tick(v.tensor_add(xout[:, sl], xown[:, sl], ps_o[:]),
                               "dve")

                @p.block.gpsimd
                def _(g, od=od):
                    p.wait(g, "dve", p.cnt["dve"])
                    p.dma(g, xo[128 * od:128 * (od + 1), :], xout[:, :], "st")
                p.barrier()

            @p.block.gpsimd
            def _(g):
                p.wait(g, "st", p.cnt["st"])

    return nc


def l1b_in_maps(x, inputs):
    WqT = np.ascontiguousarray(inputs["L1_attn_b_Wq"].T, np.float32)
    WkT = np.ascontiguousarray(inputs["L1_attn_b_Wk"].T, np.float32)
    WvT = np.ascontiguousarray(inputs["L1_attn_b_Wv"].T, np.float32)
    WoT = np.ascontiguousarray(inputs["L1_attn_b_Wo"].T, np.float32)
    brows = np.concatenate([np.asarray(inputs["L1_attn_b_" + n], np.float32)
                            for n in ("bq", "bk", "bv", "bo")])[None, :]
    slopes = alibi_slopes(16)
    NEG = -1.0e30

    kk = np.arange(128, dtype=np.float32)[:, None]
    qq = np.arange(512, dtype=np.float32)[None, :]
    pdbank = np.empty((16, 128, 5, 512), np.float32)
    for h in range(16):
        m = slopes[h]
        pdbank[h, :, 0, :] = m * (kk - qq)
        for o in range(4):
            pdf = o * 128 + kk - qq
            pdbank[h, :, 1 + o, :] = np.where(pdf <= 0, m * pdf, NEG)
    pdbank = pdbank.reshape(16, 128, 5 * 512)

    def make_ct(hc):
        ct = np.zeros((16, 4, 32), np.float32)
        for h in range(16):
            m = slopes[h]
            for jq in range(4):
                for kb in range(32):
                    if 4 * jq <= kb < 4 * jq + 4:
                        ct[h, jq, kb] = 0.0
                    elif kb < 4 * jq:
                        ct[h, jq, kb] = m * (kb * 128 - jq * 512)
                    elif kb >= 16:
                        if hc == 1:
                            ct[h, jq, kb] = m * ((kb - 16) * 128 - 2048
                                                 - jq * 512)
                        else:
                            ct[h, jq, kb] = NEG
        row = ct.reshape(1, 2048)
        return np.broadcast_to(row, (128, 2048)).copy()

    cts = [make_ct(0), make_ct(1)]

    in_maps = []
    for core in range(NCORES):
        b, h = core // 2, core % 2
        perm = core_perm(h)
        xfm = fm(x[b])[:, perm]
        m = dict(CONST_ARRAYS)
        m.update(dict(xf=np.ascontiguousarray(xfm), WqT=WqT, WkT=WkT, WvT=WvT,
                      WoT=WoT, brows=brows, pdbank=pdbank, ctbank=cts[h]))
        in_maps.append(m)

    def assemble(results):
        xn = np.empty((B, S, D), np.float32)
        for core in range(NCORES):
            b, h = core // 2, core % 2
            xn[b, h * HALF:(h + 1) * HALF, :] = results[core]["xo"].T
        return xn

    return in_maps, assemble


_BUILT = {}


def _get_prog(name, fn):
    if name not in _BUILT:
        _BUILT[name] = fn()
    return _BUILT[name]


def kernel(**inputs):
    from concourse.bass_utils import run_bass_kernel_spmd
    x = np.asarray(inputs["L0_attn_a_x"], np.float32)
    steps = [
        ("l0a", build_l0a, l0a_in_maps),
        ("l0f", build_l0f, l0f_in_maps),
        ("l0b", build_l0b, l0b_in_maps),
        ("l0f", build_l0f, l0f_in_maps),
        ("l1a", build_l1a, l1a_in_maps),
        ("l1m", build_l1m, l1m_in_maps),
        ("l1b", build_l1b, l1b_in_maps),
        ("l1m", build_l1m, l1m_in_maps),
    ]
    for name, bfn, mfn in steps:
        nc = _get_prog(name, bfn)
        in_maps, assemble = mfn(x, inputs)
        res = run_bass_kernel_spmd(nc, in_maps, core_ids=list(range(NCORES)))
        x = assemble(res.results)
    return x
